# version 03
# baseline (speedup 1.0000x reference)
from contextlib import ExitStack

import numpy as np
import ml_dtypes

import concourse.bass as bass
import concourse.mybir as mybir
from concourse.bass_utils import run_bass_kernel_spmd

F32 = mybir.dt.float32
BF16 = mybir.dt.bfloat16
I32 = mybir.dt.int32
AF = mybir.ActivationFunctionType
ALU = mybir.AluOpType
AX = mybir.AxisListType

E = 2048
H = 16
CH = 128
SEG = 2048
NCH = SEG // CH
FT = E // 128
DFF = 1408
FB = DFF // 128
NE = 18
RMS_EPS = 1e-5
LN_EPS = 1e-5
MAGIC = 0x5A5A1234
BIG = 1.0e30

_slopes = np.array([2.0 ** (-0.5 * (i + 1)) for i in range(H)], dtype=np.float64)
_r = np.exp(-_slopes)
R_C = _r ** CH
R_SEG = _r ** SEG


def _rot_tables(t0):
    freq = np.zeros(64, dtype=np.float64)
    freq[:32] = (1.0 / 1024.0) ** np.linspace(0.0, 1.0, 32)
    t = np.arange(t0, t0 + SEG, dtype=np.float64)
    theta = freq[:, None] * t[None, :]
    cos, sin = np.cos(theta), np.sin(theta)
    Cf = np.concatenate([cos, cos], axis=0).astype(np.float32)
    Sf = np.concatenate([sin, -sin], axis=0).astype(np.float32)
    return np.ascontiguousarray(Cf), np.ascontiguousarray(Sf)


def host_prep(inputs):
    x = np.ascontiguousarray(np.asarray(inputs["L0_attn_x"], dtype=np.float32))
    wq = np.ascontiguousarray(np.asarray(inputs["L0_attn_Wq"], dtype=np.float32))
    wk = np.ascontiguousarray(np.asarray(inputs["L0_attn_Wk"], dtype=np.float32))
    wv = np.ascontiguousarray(np.asarray(inputs["L0_attn_Wv"], dtype=np.float32))
    wo = np.ascontiguousarray(np.asarray(inputs["L0_attn_Wo"], dtype=np.float32))
    rms_w = np.asarray(inputs["L0_attn_rms_w"], dtype=np.float32)
    sw1 = np.asarray(inputs["L0_ffn_shared_w1"], dtype=np.float32)
    sb1 = np.asarray(inputs["L0_ffn_shared_b1"], dtype=np.float32)
    sw2 = np.asarray(inputs["L0_ffn_shared_w2"], dtype=np.float32)
    sb2 = np.asarray(inputs["L0_ffn_shared_b2"], dtype=np.float32)
    rw1 = np.asarray(inputs["L0_ffn_routed_w1"], dtype=np.float32)
    rb1 = np.asarray(inputs["L0_ffn_routed_b1"], dtype=np.float32)
    rw2 = np.asarray(inputs["L0_ffn_routed_w2"], dtype=np.float32)
    rb2 = np.asarray(inputs["L0_ffn_routed_b2"], dtype=np.float32)
    cent = np.asarray(inputs["L0_ffn_centroids"], dtype=np.float32)
    fbias = np.asarray(inputs["L0_ffn_bias"], dtype=np.float32)

    w1cat = np.concatenate([sw1, rw1], axis=0).reshape(NE, FT, 128, FB, 128)
    w1cat = np.ascontiguousarray(w1cat.transpose(0, 3, 2, 1, 4))
    w2cat = np.ascontiguousarray(np.concatenate([sw2, rw2], axis=0))
    b1T = np.ascontiguousarray(
        np.concatenate([sb1, rb1], axis=0).reshape(NE, FB, 128).transpose(0, 2, 1))
    b2cat = np.ascontiguousarray(np.concatenate([sb2, rb2], axis=0))
    centT = np.ascontiguousarray(cent.T)
    biasrep = np.ascontiguousarray(np.broadcast_to(fbias[None, :], (128, H)))
    rmswrep = np.ascontiguousarray(np.broadcast_to(rms_w[None, :], (128, E)))
    rmswT = np.ascontiguousarray(rms_w.reshape(FT, 128).T)

    i128 = np.arange(128, dtype=np.float64)
    qdecay = np.zeros((128, H * 128), dtype=np.float32)
    dmaskT = np.zeros((128, H * 128), dtype=np.float32)
    rcrep = np.zeros((128, H * 128), dtype=np.float32)
    kdec = np.zeros((128, H), dtype=np.float32)
    for h in range(H):
        qdecay[:, h * 128:(h + 1) * 128] = (_r[h] ** (i128 + 1.0))[None, :]
        diff = i128[None, :] - i128[:, None]
        dmaskT[:, h * 128:(h + 1) * 128] = np.where(
            diff >= 0, _r[h] ** np.maximum(diff, 0.0), 0.0)
        rcrep[:, h * 128:(h + 1) * 128] = R_C[h]
        kdec[:, h] = _r[h] ** (127.0 - i128)

    shared = dict(wq=wq, wk=wk, wv=wv, wo=wo, rmswrep=rmswrep, rmswT=rmswT,
                  w1cat=w1cat,
                  w2cat=w2cat, b1T=b1T, b2cat=b2cat, centT=centT,
                  biasrep=biasrep, qdecay=qdecay, dmaskT=dmaskT, rcrep=rcrep,
                  kdec=kdec,
                  identB=np.eye(128, dtype=ml_dtypes.bfloat16),
                  identF=np.eye(128, dtype=np.float32))

    in_maps = []
    for c in range(8):
        b, s = c // 4, c % 4
        x_own = np.ascontiguousarray(x[b, s * SEG:(s + 1) * SEG, :])
        es = {0: s, 1: s, 2: 0, 3: 1}[s]
        x_ex = np.ascontiguousarray(x[b, es * SEG:(es + 1) * SEG, :])
        Cown, Sown = _rot_tables(s * SEG)
        Cex, Sex = _rot_tables(es * SEG)
        coef = np.zeros((4, H), dtype=np.float64)
        if s == 1:
            coef[1, :] = 1.0
        elif s == 2:
            coef[2, :] = R_SEG
            coef[3, :] = 1.0
        elif s == 3:
            coef[1, :] = 1.0
            coef[2, :] = R_SEG
            coef[3, :] = R_SEG ** 2
        coefrep = np.ascontiguousarray(np.broadcast_to(
            coef.astype(np.float32).reshape(1, 4 * H), (128, 4 * H)))
        offs = np.zeros((1, 16), np.int32)
        offs[0, 0] = c % 2
        offs[0, 1] = 1 - (c % 2)
        m = dict(shared)
        m.update(x_own=x_own, x_ex=x_ex, rotc=Cown, rots=Sown,
                 rotc_ex=Cex, rots_ex=Sex, coefrep=coefrep, offs=offs)
        in_maps.append(m)
    return in_maps


ENGS = ("sync", "gpsimd", "tensor", "vector", "scalar")
PARANOID = False


class Cx:

    def __init__(self, nc, block):
        self.nc = nc
        self.block = block
        self.ct = {}
        self.q = {e: [] for e in ENGS}

    def now(self, sem):
        return self.ct.get(sem.name, 0)

    def bump(self, sem, n):
        self.ct[sem.name] = self.ct.get(sem.name, 0) + n
        return self.ct[sem.name]

    scope = None

    def add(self, eng, waits, fn, inc=None):
        for s, v in waits:
            assert v <= self.now(s), f"wait {s.name}>={v} exceeds appended {self.now(s)}"
        if PARANOID:
            waits = self.all_now()
        if self.scope is not None:
            fn0 = fn
            scope_name = self.scope

            def fn(e, fn0=fn0, scope_name=scope_name):
                with self.nc.named_scope(scope_name):
                    return fn0(e)
        self.q[eng].append((list(waits), fn, inc))
        if inc is not None:
            self.bump(inc[0], inc[1])

    def dma(self, eng, waits, movs, sem):
        def fn(e, movs=movs, sem=sem):
            last = None
            for o, i, kw in movs:
                last = e.dma_start(out=o, in_=i, **kw)
                last.then_inc(sem, 16)
            return None
        self.add(eng, waits, fn)
        self.bump(sem, 16 * len(movs))

    def all_now(self):
        return [(s, self.now(s)) for s in self.sems if self.now(s) > 0]

    def flush(self):
        for e in ENGS:
            steps = self.q[e]
            if not steps:
                continue
            self.q[e] = []

            def body(eng, steps=steps):
                for waits, fn, inc in steps:
                    for s, v in waits:
                        if v > 0:
                            eng.wait_ge(s, v)
                    inst = fn(eng)
                    if inc is not None:
                        inst.then_inc(inc[0], inc[1])
            getattr(self.block, e)(body)

    def barrier(self, engines=ENGS):
        waits = self.all_now()
        for e in engines:
            self.add(e, waits, lambda eng: None)


def build_program(upto="full", debug=()):
    nc = bass.Bass("TRN2", target_bir_lowering=False,
                   detect_race_conditions=False)
    di = {}

    def din(name, shape, dt=F32):
        di[name] = nc.dram_tensor(name, shape, dt, kind="ExternalInput").ap()

    din("x_own", [SEG, E]); din("x_ex", [SEG, E])
    din("wq", [E, E]); din("wk", [E, E]); din("wv", [E, E]); din("wo", [E, E])
    din("rotc", [128, SEG]); din("rots", [128, SEG])
    din("rotc_ex", [128, SEG]); din("rots_ex", [128, SEG])
    din("qdecay", [128, H * 128]); din("dmaskT", [128, H * 128])
    din("rcrep", [128, H * 128]); din("kdec", [128, H])
    din("rmswrep", [128, E]); din("rmswT", [128, FT])
    din("coefrep", [128, 4 * H])
    din("identB", [128, 128], BF16); din("identF", [128, 128])
    din("w1cat", [NE, FB, 128, FT, 128]); din("w2cat", [NE, DFF, E])
    din("b1T", [NE, 128, FB]); din("b2cat", [NE, E])
    din("centT", [E, H]); din("biasrep", [128, H])
    din("offs", [1, 16], I32)

    out_d = nc.dram_tensor("out", [SEG, E], F32, kind="ExternalOutput").ap()
    scr = dict(
        xbf=nc.dram_tensor("xbf_scr", [SEG, E], BF16, kind="Internal").ap(),
        q=nc.dram_tensor("q_scr", [NCH, 128, H * 128], BF16, kind="Internal").ap(),
        k=nc.dram_tensor("k_scr", [NCH, 128, H * 128], BF16, kind="Internal").ap(),
        v=nc.dram_tensor("v_scr", [NCH, 128, E], BF16, kind="Internal").ap(),
        outp=nc.dram_tensor("outp_scr", [NCH, 128, E], F32, kind="Internal").ap(),
        x1f=nc.dram_tensor("x1f_scr", [SEG, E], F32, kind="Internal").ap(),
        x1hi=nc.dram_tensor("x1hi_scr", [SEG, E], BF16, kind="Internal").ap(),
        x1lo=nc.dram_tensor("x1lo_scr", [SEG, E], BF16, kind="Internal").ap(),
        bex=nc.dram_tensor("bex_scr", [128, H * 128], F32, kind="Internal").ap(),
    )
    shm = nc.dram_tensor("shm", [2, 2 * 128, H * 128], F32, kind="Internal",
                         addr_space="Shared")
    flg = nc.dram_tensor("flg", [2, 16], I32, kind="Internal",
                         addr_space="Shared")

    dbg = {}
    if debug:
        def dout(name, shape, dt=F32):
            dbg[name] = nc.dram_tensor(name, shape, dt, kind="ExternalOutput").ap()
        if "qkv" in debug:
            dout("d_q", [NCH, 128, H * 128], BF16)
            dout("d_k", [NCH, 128, H * 128], BF16)
            dout("d_v", [NCH, 128, E], BF16)
        if "scan" in debug:
            dout("d_outp", [NCH, 128, E])
            dout("d_Aown", [128, H * 128])
            dout("d_Bex", [128, H * 128])
        if "sin" in debug:
            dout("d_Sin", [128, H * 128])
        if "attn" in debug:
            dout("d_attn", [NCH, 128, E])
        if "x1" in debug:
            dout("d_x1", [SEG, E])
        if "gates" in debug:
            dout("d_scores", [128, NCH * H])
            dout("d_gates", [128, NCH * NE])

    with ExitStack() as top:
        block = top.enter_context(nc.Block())
        cx = Cx(nc, block)
        cx.di, cx.scr, cx.dbg, cx.out_d = di, scr, dbg, out_d
        cx.shm, cx.flg = shm, flg

        sem = lambda n: top.enter_context(nc.semaphore(n))
        cx.LD = sem("sLD")
        cx.GW = sem("sGW")
        cx.PE = sem("sPE")
        cx.DV = sem("sDV")
        cx.AC = sem("sAC")
        cx.GP = sem("sGP")
        cx.ST = sem("sST")
        cx.PO = sem("sPO")
        cx.sems = [cx.LD, cx.GW, cx.PE, cx.DV, cx.AC, cx.GP, cx.ST]

        sb = lambda n, shp, dt=F32: top.enter_context(nc.sbuf_tensor(n, shp, dt))
        P = {}
        P["qdec"] = sb("c_qdec", [128, H * 128])
        P["kdec"] = sb("c_kdec", [128, H])
        P["coefr"] = sb("c_coefr", [128, 4 * H])
        P["biasrep"] = sb("c_biasr", [128, H])
        P["identB"] = sb("c_identB", [128, 128], BF16)
        P["identF"] = sb("c_identF", [128, 128])
        P["offs"] = sb("c_offs", [1, 16], I32)
        P["magic"] = sb("c_magic", [1, 16], I32)
        P["flagrd"] = sb("c_flagrd", [1, 16], I32)
        P["kvst"] = sb("c_kvst", [128, H * 128])
        P["kvbf"] = sb("c_kvbf", [128, H * 128], BF16)
        cx.P = P

        ps = lambda n, shp, dt=F32: top.enter_context(nc.psum_tensor(n, shp, dt))
        cx.pa = [ps("ps_a0", [128, 512]), ps("ps_a1", [128, 512])]
        cx.pb = [ps("ps_b0", [128, 512]), ps("ps_b1", [128, 512])]
        cx.pc = [ps("ps_c0", [128, 512]), ps("ps_c1", [128, 512])]
        cx.ptb = ps("ps_tb", [128, 1024], BF16)
        cx.ptf = ps("ps_tf", [128, 512])

        phase_consts(cx)

        with ExitStack() as front:
            fb = lambda n, shp, dt=F32: front.enter_context(nc.sbuf_tensor(n, shp, dt))
            F = {}
            F["xfm"] = fb("f_xfm", [128, FT * SEG], BF16)
            F["wbuf"] = fb("f_wbuf", [128, FT * E], BF16)
            F["rotC"] = fb("f_rotC", [128, SEG])
            F["rotS"] = fb("f_rotS", [128, SEG])
            F["ptile"] = [fb(f"f_pt{i}", [128, 512], BF16) for i in range(2)]
            F["rt_t"] = fb("f_rtt", [128, 512])
            F["rt_u"] = fb("f_rtu", [128, 512])
            F["vstage"] = [fb(f"f_vs{i}", [128, E], BF16) for i in range(2)]
            F["khat"] = [fb(f"f_kh{i}", [128, 128], BF16) for i in range(4)]
            cx.F = F

            with ExitStack() as s2s:
                F["kgr"] = s2s.enter_context(nc.sbuf_tensor("f_kgr", [128, 8 * 512], BF16))
                F["Bex"] = s2s.enter_context(nc.sbuf_tensor("f_Bex", [128, H * 128], F32))
                cx.scope = "ingest_ex"
                phase_ingest(cx, di["x_ex"], "rotc_ex", "rots_ex")
                cx.scope = "s2"
                if upto != "ingest":
                    phase_s2(cx)
                cx.flush()
            if upto != "ingest":
                cx.scope = "ingest_own"
                phase_ingest(cx, di["x_own"], "rotc", "rots")
                cx.scope = "proj"
                phase_proj_pipeline(cx)
            cx.flush()

        if debug and "qkv" in debug and upto not in ("ingest",):
            _dump(cx, dbg["d_q"], scr["q"])
            _dump(cx, dbg["d_k"], scr["k"])
            _dump(cx, dbg["d_v"], scr["v"])
        if debug and "scan" in debug and upto == "s2only":
            pass

        if upto in ("scan", "exch", "co", "full"):
            co_outer = ExitStack()
            sbn_o = lambda n, shp, dt=F32: co_outer.enter_context(
                nc.sbuf_tensor(n, shp, dt))
            Q = {}
            Q["wbuf"] = sbn_o("o_wbuf", [128, FT * E], BF16)
            Q["rcrep"] = sbn_o("o_rcrep", [128, H * 128])
            Q["rmswT"] = sbn_o("o_rmswT", [128, FT])
            Q["Sin"] = sbn_o("o_Sin", [128, H * 128])
            Q["Sbf"] = sbn_o("o_Sbf", [128, H * 128], BF16)
            cx.Q = Q
            wosrc = bass.AP(di["wo"].tensor, 0,
                            [[2048, 128], [128 * 2048, FT], [1, 2048]])
            cx.dma("gpsimd", cx.all_now(), [(Q["wbuf"][:, :], wosrc, {})], cx.GW)
            cx.co_gw_tot = cx.now(cx.GW)
            cx.dma("sync", cx.all_now(),
                   [(Q["rcrep"][:], di["rcrep"][:, :], {}),
                    (Q["rmswT"][:], di["rmswT"][:, :], {})], cx.LD)
            with ExitStack() as sc:
                sbn = lambda n, shp, dt=F32: sc.enter_context(nc.sbuf_tensor(n, shp, dt))
                S = {}
                S["qch"] = [sbn(f"s_q{i}", [128, H * 128], BF16) for i in range(2)]
                S["kch"] = [sbn(f"s_k{i}", [128, H * 128], BF16) for i in range(2)]
                S["vch"] = [sbn(f"s_v{i}", [128, E], BF16) for i in range(2)]
                S["khat"] = [sbn(f"s_kh{i}", [128, 128], BF16) for i in range(4)]
                S["sprm"] = [sbn(f"s_sp{i}", [128, 128], BF16) for i in range(4)]
                S["qhat"] = [sbn(f"s_qh{i}", [128, 128], BF16) for i in range(4)]
                S["outbuf"] = [sbn(f"s_ob{i}", [128, E]) for i in range(2)]
                S["dmaskT"] = sbn("s_dmaskT", [128, H * 128])
                cx.S = S
                cx.scope = "scan"
                phase_scan(cx)
                cx.flush()
            if debug and "scan" in debug:
                _dump(cx, dbg["d_outp"], scr["outp"])
                _dump_sbuf(cx, dbg["d_Aown"], P["kvst"])
                _dump(cx, dbg["d_Bex"], scr["bex"])

        if upto in ("exch", "co", "full"):
            with ExitStack() as co:
                sbn = lambda n, shp, dt=F32: co.enter_context(nc.sbuf_tensor(n, shp, dt))
                cx.scope = "exch"
                phase_exchange(cx)
                if debug and "sin" in debug:
                    _dump_sbuf(cx, dbg["d_Sin"], Q["Sin"])
                Q["opb"] = [sbn(f"o_op{i}", [128, E]) for i in range(4)]
                Q["qch"] = [sbn(f"o_q{i}", [128, H * 128], BF16) for i in range(2)]
                Q["qhat"] = [sbn(f"o_qh{i}", [128, 128], BF16) for i in range(4)]
                Q["rmsbf"] = [sbn(f"o_rmsbf{i}", [128, E], BF16) for i in range(2)]
                Q["rmsfm"] = sbn("o_rmsfm", [128, H * 128], BF16)
                Q["zbuf"] = [sbn(f"o_zbuf{i}", [128, E]) for i in range(2)]
                Q["TZ"] = sbn("o_TZ", [128, E])
                Q["Dd"] = sbn("o_Dd", [128, E], BF16)
                Q["x1b"] = sbn("o_x1b", [128, E])
                Q["hibuf"] = sbn("o_hibuf", [128, E], BF16)
                Q["lobuf"] = sbn("o_lobuf", [128, E], BF16)
                Q["cols"] = sbn("o_cols", [128, 32])
                if upto in ("co", "full"):
                    cx.scope = "co"
                    phase_co(cx, debug)
                cx.flush()
            co_outer.close()
            if upto in ("co", "full") and debug and "x1" in debug:
                _dump(cx, dbg["d_x1"], scr["x1f"])

        if upto == "full":
            cx.scope = "ffn"
            phase_ffn(cx, debug)

        cx.add("gpsimd", cx.all_now(), lambda eng: None)
        cx.flush()

    nc.finalize()
    return nc



def phase_consts(cx):
    di, P = cx.di, cx.P
    movs = [(P[t][:], di[n][:, :], {}) for n, t in
            [("qdecay", "qdec"), ("kdec", "kdec"), ("coefrep", "coefr"),
             ("biasrep", "biasrep"),
             ("identB", "identB"), ("identF", "identF"), ("offs", "offs")]]
    cx.dma("sync", [], movs, cx.LD)

    def ve(v):
        v.memset(cx.P["magic"][:], MAGIC)
        v.memset(cx.P["kvst"][:], 0.0)
        return v.memset(cx.P["kvbf"][:], 0.0)
    cx.add("vector", [], ve, inc=(cx.DV, 1))
    cx.barrier()


def phase_ingest(cx, x_ap, rotc_name, rots_name):
    F, scr, di = cx.F, cx.scr, cx.di
    cx.dma("gpsimd", cx.all_now(), [(scr["xbf"][:, :], x_ap[:, :], {})], cx.GW)
    gw_tot = cx.now(cx.GW)
    movs = [(F["rotC"][:], di[rotc_name][:, :], {}),
            (F["rotS"][:], di[rots_name][:, :], {})]
    movs += [(F["xfm"][:, t * SEG:(t + 1) * SEG],
              scr["xbf"][:, t * 128:(t + 1) * 128],
              {"transpose": True}) for t in range(FT)]
    cx.dma("sync", [(cx.GW, gw_tot)], movs, cx.LD)
    cx.barrier()


def _rot_evac(v, cx, psrc, dst, g0):
    F = cx.F
    cs = slice(g0, g0 + 512)
    v.tensor_tensor(F["rt_t"][:], psrc[:, :], F["rotC"][:, cs], ALU.mult)
    v.tensor_tensor(F["rt_u"][0:64, :], psrc[64:128, :], F["rotS"][0:64, cs], ALU.mult)
    v.tensor_tensor(F["rt_u"][64:128, :], psrc[0:64, :], F["rotS"][64:128, cs], ALU.mult)
    return v.tensor_tensor(dst[:], F["rt_t"][:], F["rt_u"][:], ALU.add)


def _w_full_load(cx, wname):
    src = bass.AP(cx.di[wname].tensor, 0, [[2048, 128], [128 * 2048, FT], [1, 2048]])
    cx.dma("gpsimd", cx.all_now(), [(cx.F["wbuf"][:, :], src, {})], cx.GW)
    return cx.now(cx.GW)


def phase_proj_pipeline(cx):
    F, scr, di = cx.F, cx.scr, cx.di
    HFT = FT * 1024

    def load_half(wname, hh, slot, waits):
        src = bass.AP(di[wname].tensor, hh * 1024,
                      [[2048, 128], [128 * 2048, FT], [1, 1024]])
        cx.dma("gpsimd", waits,
               [(F["wbuf"][:, slot * HFT:(slot + 1) * HFT], src, {})], cx.GW)
        return cx.now(cx.GW)

    gw_q = [load_half("wq", 0, 0, cx.all_now()), load_half("wq", 1, 1, [])]
    gw_k = [None, None]
    gw_v = [None, None]
    pe_half_end = {}

    def fm_half(wname, hh, slot, dst_scr, gw_tot):
        pe0, dv0, st0 = cx.now(cx.PE), cx.now(cx.DV), cx.now(cx.ST)
        for j in range(32):
            grp, h8 = divmod(j, 8)
            h = hh * 8 + h8
            w = [(cx.GW, gw_tot)] if j == 0 else []
            if j >= 2:
                w.append((cx.DV, dv0 + j - 1))
            elif j < 2:
                w.append((cx.DV, dv0))

            def pe(t, j=j, grp=grp, h8=h8, slot=slot):
                bank = cx.pa[j % 2]
                for ti in range(FT):
                    mm = t.matmul(
                        bank[:, :],
                        F["wbuf"][:, slot * HFT + ti * 1024 + h8 * 128:
                                  slot * HFT + ti * 1024 + h8 * 128 + 128],
                        F["xfm"][:, ti * SEG + grp * 512:
                                 ti * SEG + grp * 512 + 512],
                        start=(ti == 0), stop=(ti == FT - 1))
                return mm
            cx.add("tensor", w, pe, inc=(cx.PE, 1))

            wv = [(cx.PE, pe0 + j + 1)]
            if j >= 2:
                wv.append((cx.ST, st0 + 16 * (j - 1)))
            elif j < 2:
                wv.append((cx.ST, st0))

            def ve(v, j=j, grp=grp):
                return _rot_evac(v, cx, cx.pa[j % 2], F["ptile"][j % 2],
                                 grp * 512)
            cx.add("vector", wv, ve, inc=(cx.DV, 1))

            off = (grp * 4) * (128 * H * 128) + h * 128
            dap = bass.AP(dst_scr.tensor, off,
                          [[H * 128, 128], [128 * H * 128, 4], [1, 128]])
            cx.dma("sync", [(cx.DV, dv0 + j + 1)],
                   [(dap, F["ptile"][j % 2][:], {})], cx.ST)
        return cx.now(cx.PE)

    e = fm_half("wq", 0, 0, scr["q"], gw_q[0])
    gw_k[0] = load_half("wk", 0, 0, [(cx.PE, e)])
    e2 = fm_half("wq", 1, 1, scr["q"], gw_q[1])
    gw_k[1] = load_half("wk", 1, 1, [(cx.PE, e2)])
    e3 = fm_half("wk", 0, 0, scr["k"], gw_k[0])
    gw_v[0] = load_half("wv", 0, 0, [(cx.PE, e3)])
    e4 = fm_half("wk", 1, 1, scr["k"], gw_k[1])
    gw_v[1] = load_half("wv", 1, 1, [(cx.PE, e4)])

    banks = [cx.pa[0], cx.pa[1], cx.pb[0], cx.pb[1]]
    pe0, ac0, st0, dv0 = cx.now(cx.PE), cx.now(cx.AC), cx.now(cx.ST), cx.now(cx.DV)
    for n in range(NCH):
        w = []
        if n == 0:
            w = [(cx.GW, gw_v[1]), (cx.DV, dv0)]
        if n >= 1:
            w.append((cx.AC, ac0 + n))

        def pe(t, n=n):
            for eb in range(4):
                slot, e2_ = divmod(eb, 2)
                for ti in range(FT):
                    mm = t.matmul(
                        banks[eb][:, :],
                        F["xfm"][:, ti * SEG + n * 128: ti * SEG + n * 128 + 128],
                        F["wbuf"][:, slot * HFT + ti * 1024 + e2_ * 512:
                                  slot * HFT + ti * 1024 + e2_ * 512 + 512],
                        start=(ti == 0), stop=(ti == FT - 1))
            return mm
        cx.add("tensor", w, pe, inc=(cx.PE, 1))

        wa = [(cx.PE, pe0 + n + 1)]
        if n >= 2:
            wa.append((cx.ST, st0 + 16 * (n - 1)))

        def ac(s, n=n):
            for eb in range(4):
                cp = s.copy(F["vstage"][n % 2][:, eb * 512:(eb + 1) * 512],
                            banks[eb][:, :])
            return cp
        cx.add("scalar", wa, ac, inc=(cx.AC, 1))

        cx.dma("sync", [(cx.AC, ac0 + n + 1)],
               [(scr["v"][n, :, :], F["vstage"][n % 2][:], {})], cx.ST)
    cx.barrier()


def phase_proj(cx, wname, dst_scr, rotary):
    F, pa = cx.F, cx.pa
    gw_tot = _w_full_load(cx, wname)
    pe0, dv0, st0 = cx.now(cx.PE), cx.now(cx.DV), cx.now(cx.ST)

    for j in range(64):
        grp, h = divmod(j, 16)
        w = [(cx.GW, gw_tot)] if j == 0 else []
        if j >= 2:
            w.append((cx.DV, dv0 + j - 1))

        def pe(t, j=j, grp=grp, h=h):
            bank = cx.pa[j % 2]
            for ti in range(FT):
                mm = t.matmul(bank[:, :],
                              F["wbuf"][:, ti * E + h * 128: ti * E + h * 128 + 128],
                              F["xfm"][:, ti * SEG + grp * 512:
                                       ti * SEG + grp * 512 + 512],
                              start=(ti == 0), stop=(ti == FT - 1))
            return mm
        cx.add("tensor", w, pe, inc=(cx.PE, 1))

        wv = [(cx.PE, pe0 + j + 1)]
        if j >= 2:
            wv.append((cx.ST, st0 + 16 * (j - 1)))

        def ve(v, j=j, grp=grp, rotary=rotary):
            if rotary:
                return _rot_evac(v, cx, cx.pa[j % 2], F["ptile"][j % 2], grp * 512)
            return v.tensor_copy(F["ptile"][j % 2][:], cx.pa[j % 2][:, :])
        cx.add("vector", wv, ve, inc=(cx.DV, 1))

        off = (grp * 4) * (128 * H * 128) + h * 128
        dap = bass.AP(dst_scr.tensor, off,
                      [[H * 128, 128], [128 * H * 128, 4], [1, 128]])
        cx.dma("sync", [(cx.DV, dv0 + j + 1)],
               [(dap, F["ptile"][j % 2][:], {})], cx.ST)
    cx.barrier()


def phase_projv(cx):
    F, scr = cx.F, cx.scr
    gw_tot = _w_full_load(cx, "wv")
    pe0, ac0, st0 = cx.now(cx.PE), cx.now(cx.AC), cx.now(cx.ST)
    banks = [cx.pa[0], cx.pa[1], cx.pb[0], cx.pb[1]]

    for n in range(NCH):
        w = [(cx.GW, gw_tot)] if n == 0 else []
        if n >= 1:
            w.append((cx.AC, ac0 + n))

        def pe(t, n=n):
            for eb in range(4):
                for ti in range(FT):
                    mm = t.matmul(banks[eb][:, :],
                                  F["xfm"][:, ti * SEG + n * 128: ti * SEG + n * 128 + 128],
                                  F["wbuf"][:, ti * E + eb * 512: ti * E + eb * 512 + 512],
                                  start=(ti == 0), stop=(ti == FT - 1))
            return mm
        cx.add("tensor", w, pe, inc=(cx.PE, 1))

        wa = [(cx.PE, pe0 + n + 1)]
        if n >= 2:
            wa.append((cx.ST, st0 + 16 * (n - 1)))

        def ac(s, n=n):
            for eb in range(4):
                cp = s.copy(F["vstage"][n % 2][:, eb * 512:(eb + 1) * 512],
                            banks[eb][:, :])
            return cp
        cx.add("scalar", wa, ac, inc=(cx.AC, 1))

        cx.dma("sync", [(cx.AC, ac0 + n + 1)],
               [(scr["v"][n, :, :], F["vstage"][n % 2][:], {})], cx.ST)
    cx.barrier()


def phase_s2(cx):
    F, P = cx.F, cx.P

    def ve0(v):
        return v.memset(F["Bex"][:], 0.0)
    cx.add("vector", [], ve0, inc=(cx.DV, 1))
    for hh in range(2):
        movs = []
        for wn, base in [("wk", 0), ("wv", FT * 1024)]:
            src = bass.AP(cx.di[wn].tensor, hh * 1024,
                          [[2048, 128], [128 * 2048, FT], [1, 1024]])
            movs.append((F["wbuf"][:, base:base + FT * 1024], src, {}))
        cx.dma("gpsimd", cx.all_now(), movs, cx.GW)
        gw_tot = cx.now(cx.GW)

        for grp in range(4):
            dv_k0 = cx.now(cx.DV)
            for j in range(8):
                w = [(cx.GW, gw_tot)] if (grp == 0 and j == 0) else []
                if j >= 2:
                    w.append((cx.DV, dv_k0 + j - 1))

                def pe(t, j=j, grp=grp, hh=hh):
                    bank = cx.pa[j % 2]
                    for ti in range(FT):
                        mm = t.matmul(
                            bank[:, :],
                            F["wbuf"][:, ti * 1024 + j * 128: ti * 1024 + j * 128 + 128],
                            F["xfm"][:, ti * SEG + grp * 512: ti * SEG + grp * 512 + 512],
                            start=(ti == 0), stop=(ti == FT - 1))
                    return mm
                cx.add("tensor", w, pe, inc=(cx.PE, 1))

                def ve(v, j=j, grp=grp):
                    return _rot_evac(v, cx, cx.pa[j % 2],
                                     F["kgr"][:, j * 512:(j + 1) * 512], grp * 512)
                cx.add("vector", [(cx.PE, cx.now(cx.PE))], ve, inc=(cx.DV, 1))
            dv_kdone = cx.now(cx.DV)

            for cc in range(4):
                n = grp * 4 + cc
                def pe_v(t, n=n):
                    for vb in range(2):
                        bank = cx.pb[vb]
                        for ti in range(FT):
                            mm = t.matmul(
                                bank[:, :],
                                F["xfm"][:, ti * SEG + n * 128: ti * SEG + n * 128 + 128],
                                F["wbuf"][:, FT * 1024 + ti * 1024 + vb * 512:
                                          FT * 1024 + ti * 1024 + vb * 512 + 512],
                                start=(ti == 0), stop=(ti == FT - 1))
                    return mm
                wv = [(cx.AC, cx.now(cx.AC))]
                cx.add("tensor", wv, pe_v, inc=(cx.PE, 1))

                def ac_v(s, cc=cc):
                    s.copy(F["vstage"][cc % 2][:, 0:512], cx.pb[0][:, :])
                    return s.copy(F["vstage"][cc % 2][:, 512:1024], cx.pb[1][:, :])
                cx.add("scalar", [(cx.PE, cx.now(cx.PE))], ac_v, inc=(cx.AC, 1))
                ac_v_tot = cx.now(cx.AC)

                for rr in range(2):
                    j0 = rr * 4

                    def pe_t(t, j0=j0, cc=cc, rr=rr):
                        for ri in range(4):
                            j = j0 + ri
                            tr = t.transpose(
                                cx.ptb[:, (rr * 4 + ri) * 128:(rr * 4 + ri) * 128 + 128],
                                F["kgr"][:, j * 512 + cc * 128: j * 512 + cc * 128 + 128],
                                P["identB"][:])
                        return tr
                    cx.add("tensor", [(cx.DV, dv_kdone), (cx.AC, cx.now(cx.AC))],
                           pe_t, inc=(cx.PE, 1))

                    def ac_k(s, j0=j0, rr=rr, hh=hh):
                        for ri in range(4):
                            h = hh * 8 + j0 + ri
                            cp = s.activation(
                                F["khat"][ri][:],
                                cx.ptb[:, (rr * 4 + ri) * 128:(rr * 4 + ri) * 128 + 128],
                                AF.Copy, scale=P["kdec"][:, h:h + 1])
                        return cp
                    cx.add("scalar", [(cx.PE, cx.now(cx.PE))], ac_k, inc=(cx.AC, 1))

                    def pe_s(t, j0=j0, rr=rr, cc=cc):
                        bank = cx.pc[rr]
                        for ri in range(4):
                            j = j0 + ri
                            mm = t.matmul(bank[:, ri * 128:(ri + 1) * 128],
                                          F["khat"][ri][:],
                                          F["vstage"][cc % 2][:, j * 128:(j + 1) * 128],
                                          start=(ri == 0), stop=(ri == 3))
                        return mm
                    cx.add("tensor",
                           [(cx.AC, cx.now(cx.AC)), (cx.DV, cx.now(cx.DV))],
                           pe_s, inc=(cx.PE, 1))

                    def dv_b(v, j0=j0, rr=rr, hh=hh):
                        bank = cx.pc[rr]
                        for ri in range(4):
                            h = hh * 8 + j0 + ri
                            st = v.scalar_tensor_tensor(
                                F["Bex"][:, h * 128:(h + 1) * 128],
                                F["Bex"][:, h * 128:(h + 1) * 128],
                                float(R_C[h]),
                                bank[:, ri * 128:(ri + 1) * 128],
                                ALU.mult, ALU.add)
                        return st
                    cx.add("vector", [(cx.PE, cx.now(cx.PE))], dv_b, inc=(cx.DV, 1))
    cx.dma("sync", [(cx.DV, cx.now(cx.DV))],
           [(cx.scr["bex"][:, :], F["Bex"][:], {})], cx.ST)
    cx.barrier()


def _dump(cx, dst_ap, src_scr):
    cx.dma("gpsimd", cx.all_now(), [(dst_ap[:], src_scr[:], {})], cx.GW)
    cx.barrier(("gpsimd",))


def _dump_sbuf(cx, dst_ap, src_sb):
    cx.dma("gpsimd", cx.all_now(), [(dst_ap[:, :], src_sb[:], {})], cx.GW)
    cx.barrier(("gpsimd",))


def phase_scan(cx):
    S, P, scr = cx.S, cx.P, cx.scr
    cx.dma("sync", [], [(S["dmaskT"][:], cx.di["dmaskT"][:, :], {})], cx.LD)
    cx.barrier(("sync", "tensor", "vector", "scalar", "gpsimd"))
    ld0, st0 = cx.now(cx.LD), cx.now(cx.ST)
    pe0, dv0, ac0, gp0 = (cx.now(cx.PE), cx.now(cx.DV), cx.now(cx.AC),
                          cx.now(cx.GP))

    def loads(n):
        w = []
        if n >= 2:
            w = [(cx.PE, pe0 + 2 * (4 * (n - 2) + 4)),
                 (cx.DV, dv0 + 3 * (4 * (n - 2) + 3) + 1)]
        movs = [(S["qch"][n % 2][:], scr["q"][n, :, :], {}),
                (S["kch"][n % 2][:], scr["k"][n, :, :], {}),
                (S["vch"][n % 2][:], scr["v"][n, :, :], {})]
        cx.dma("sync", w, movs, cx.LD)

    loads(0)
    loads(1)
    for n in range(NCH):
        for r in range(4):
            m = n * 4 + r
            w = [(cx.LD, ld0 + 48 * (n + 1))]
            if m >= 2:
                w += [(cx.DV, dv0 + 3 * (m - 2) + 1),
                      (cx.AC, ac0 + m - 1)]

            def pe_a(t, n=n, r=r):
                kb, qb = S["kch"][n % 2], S["qch"][n % 2]
                bank = cx.pa[r % 2]
                for ri in range(4):
                    h = r * 4 + ri
                    mm = t.matmul(bank[:, ri * 128:(ri + 1) * 128],
                                  kb[:, h * 128:(h + 1) * 128],
                                  qb[:, h * 128:(h + 1) * 128],
                                  start=(ri == 0), stop=(ri == 3))
                half = (r % 2) * 512
                for ri in range(4):
                    h = r * 4 + ri
                    tr = t.transpose(cx.ptb[:, half + ri * 128: half + ri * 128 + 128],
                                     kb[:, h * 128:(h + 1) * 128],
                                     P["identB"][:])
                return tr
            cx.add("tensor", w, pe_a, inc=(cx.PE, 1))

            def dv_a(v, n=n, r=r):
                bank = cx.pa[r % 2]
                for ri in range(4):
                    h = r * 4 + ri
                    v.tensor_tensor(S["sprm"][ri][:],
                                    bank[:, ri * 128:(ri + 1) * 128],
                                    S["dmaskT"][:, h * 128:(h + 1) * 128],
                                    ALU.mult)
                qb = S["qch"][n % 2]
                for ri in range(4):
                    h = r * 4 + ri
                    tt = v.tensor_tensor(S["qhat"][ri][:],
                                         qb[:, h * 128:(h + 1) * 128],
                                         P["qdec"][:, h * 128:(h + 1) * 128],
                                         ALU.mult)
                return tt
            cx.add("vector", [(cx.PE, pe0 + 2 * m + 1)], dv_a, inc=(cx.DV, 1))

            def ac_k(sc, r=r):
                half = (r % 2) * 512
                for ri in range(4):
                    h = r * 4 + ri
                    cp = sc.activation(S["khat"][ri][:],
                                       cx.ptb[:, half + ri * 128: half + ri * 128 + 128],
                                       AF.Copy, scale=P["kdec"][:, h:h + 1])
                return cp
            cx.add("scalar", [(cx.PE, pe0 + 2 * m + 1)], ac_k, inc=(cx.AC, 1))

            wb = [(cx.DV, dv0 + 3 * m + 1), (cx.AC, ac0 + m + 1)]
            if m >= 4:
                wb.append((cx.GP, gp0 + m - 3))

            def pe_b(t, n=n, r=r):
                vb = S["vch"][n % 2]
                ob = cx.pb[r % 2]
                for ri in range(4):
                    h = r * 4 + ri
                    t.matmul(ob[:, ri * 128:(ri + 1) * 128],
                             S["qhat"][ri][:],
                             P["kvbf"][:, h * 128:(h + 1) * 128],
                             start=(ri == 0), stop=False)
                    t.matmul(ob[:, ri * 128:(ri + 1) * 128],
                             S["sprm"][ri][:],
                             vb[:, h * 128:(h + 1) * 128],
                             start=False, stop=(ri == 3))
                kb_ = cx.pc[r % 2]
                for ri in range(4):
                    h = r * 4 + ri
                    mm = t.matmul(kb_[:, ri * 128:(ri + 1) * 128],
                                  S["khat"][ri][:],
                                  vb[:, h * 128:(h + 1) * 128],
                                  start=(ri == 0), stop=(ri == 3))
                return mm
            cx.add("tensor", wb, pe_b, inc=(cx.PE, 1))

            def dv_kv(v, r=r):
                kb_ = cx.pc[r % 2]
                for ri in range(4):
                    h = r * 4 + ri
                    st = v.scalar_tensor_tensor(
                        P["kvst"][:, h * 128:(h + 1) * 128],
                        P["kvst"][:, h * 128:(h + 1) * 128],
                        float(R_C[h]),
                        kb_[:, ri * 128:(ri + 1) * 128],
                        ALU.mult, ALU.add)
                return st
            cx.add("vector", [(cx.PE, pe0 + 2 * m + 2)], dv_kv, inc=(cx.DV, 1))

            wo_ = [(cx.PE, pe0 + 2 * m + 2)]
            if r == 0 and n >= 2:
                wo_.append((cx.ST, st0 + 16 * (n - 1)))

            def dv_out(v, n=n, r=r):
                return v.tensor_copy(S["outbuf"][n % 2][:, r * 512:(r + 1) * 512],
                                     cx.pb[r % 2][:, :])
            cx.add("vector", wo_, dv_out, inc=(cx.DV, 1))

            def gp_kv(g, r=r):
                for ri in range(4):
                    h = r * 4 + ri
                    cp = g.tensor_copy(P["kvbf"][:, h * 128:(h + 1) * 128],
                                       P["kvst"][:, h * 128:(h + 1) * 128])
                return cp
            cx.add("gpsimd", [(cx.DV, dv0 + 3 * m + 2)], gp_kv, inc=(cx.GP, 1))

        cx.dma("sync", [(cx.DV, dv0 + 3 * (4 * n + 4))],
               [(scr["outp"][n, :, :], S["outbuf"][n % 2][:], {})], cx.ST)
        if n + 2 < NCH:
            loads(n + 2)
    cx.barrier()


def phase_exchange(cx):
    P, Q = cx.P, cx.Q
    shm, flg = cx.shm, cx.flg
    waits = cx.all_now()
    gw_base = cx.now(cx.GW)

    def gp(g, gw_base=gw_base):
        with g.register("my") as my, g.register("pa") as pa, \
             g.register("fl") as fl, g.register("nd") as nd, \
             g.register("cnt") as cnt, g.register("off") as off:
            g.reg_load(my, cx.P["offs"][:1, 0:1])
            g.reg_load(pa, cx.P["offs"][:1, 1:2])
            g.reg_alu(off, my, 2 * 128 * H * 128, ALU.mult)
            g.dma_start(out=bass.AP(shm, off, [[H * 128, 128], [1, H * 128]]),
                        in_=P["kvst"][:]).then_inc(cx.GW, 16)
            g.reg_alu(off, off, 128 * H * 128, ALU.add)
            g.dma_start(out=bass.AP(shm, off, [[H * 128, 128], [1, H * 128]]),
                        in_=cx.scr["bex"][:, :]).then_inc(cx.GW, 16)
            g.wait_ge(cx.GW, gw_base + 32)
            g.reg_alu(off, my, 16, ALU.mult)
            g.dma_start(out=bass.AP(flg, off, [[16, 1], [1, 16]]),
                        in_=P["magic"][:]).then_inc(cx.GW, 16)
            g.wait_ge(cx.GW, gw_base + 48)
            g.reg_mov(nd, 1)
            g.reg_mov(cnt, 0)
            g.reg_alu(off, pa, 16, ALU.mult)
            with g.While(nd):
                g.reg_add(cnt, cnt, 16)
                g.dma_start(out=P["flagrd"][:],
                            in_=bass.AP(flg, off, [[16, 1], [1, 16]])
                            ).then_inc(cx.PO, 16)
                g.wait_ge(cx.PO, cnt)
                g.reg_load(fl, P["flagrd"][:1, 0:1])
                g.reg_alu(nd, fl, MAGIC, ALU.not_equal)
            g.reg_alu(off, pa, 2 * 128 * H * 128, ALU.mult)
            g.dma_start(out=cx.pbuf[:, 0:H * 128],
                        in_=bass.AP(shm, off, [[H * 128, 128], [1, H * 128]])
                        ).then_inc(cx.GW, 16)
            g.reg_alu(off, off, 128 * H * 128, ALU.add)
            g.dma_start(out=cx.pbuf[:, H * 128:2 * H * 128],
                        in_=bass.AP(shm, off, [[H * 128, 128], [1, H * 128]])
                        ).then_inc(cx.GW, 16)
        return None
    HK = H * 128
    with ExitStack() as ex:
        pbuf = ex.enter_context(cx.nc.sbuf_tensor("e_pbuf", [128, 3 * HK], F32))
        cx.pbuf = pbuf
        cx.add("gpsimd", waits, gp)
        cx.bump(cx.GW, 16 * 5)
        cx.dma("gpsimd", [], [(pbuf[:, 2 * HK:3 * HK], cx.scr["bex"][:, :], {})],
               cx.GW)
        gw_tot = cx.now(cx.GW)

        def dv0(v):
            return v.memset(Q["Sin"][:], 0.0)
        cx.add("vector", [(cx.GW, gw_tot)], dv0, inc=(cx.DV, 1))
        for si in range(4):
            def dvc(v, si=si):
                for h in range(H):
                    if si == 0:
                        src = P["kvst"][:, h * 128:(h + 1) * 128]
                    elif si == 1:
                        src = pbuf[:, h * 128:(h + 1) * 128]
                    elif si == 2:
                        src = pbuf[:, 2 * HK + h * 128: 2 * HK + (h + 1) * 128]
                    else:
                        src = pbuf[:, HK + h * 128: HK + (h + 1) * 128]
                    st = v.scalar_tensor_tensor(
                        Q["Sin"][:, h * 128:(h + 1) * 128], src,
                        P["coefr"][:, si * H + h: si * H + h + 1],
                        Q["Sin"][:, h * 128:(h + 1) * 128],
                        ALU.mult, ALU.add)
                return st
            cx.add("vector", [(cx.DV, cx.now(cx.DV))], dvc, inc=(cx.DV, 1))

        def ac(s):
            return s.copy(Q["Sbf"][:], Q["Sin"][:])
        cx.add("scalar", [(cx.DV, cx.now(cx.DV))], ac, inc=(cx.AC, 1))
        cx.barrier()
        cx.flush()


def phase_co(cx, debug=()):
    P, Q, scr, di = cx.P, cx.Q, cx.scr, cx.di
    gw_tot = cx.co_gw_tot
    ld0 = cx.now(cx.LD)

    def dv_wscale(v):
        for ti in range(FT):
            ts = v.tensor_scalar(Q["wbuf"][:, ti * E:(ti + 1) * E],
                                 Q["wbuf"][:, ti * E:(ti + 1) * E],
                                 Q["rmswT"][:, ti:ti + 1], None, ALU.mult)
        return ts
    cx.add("vector", [(cx.GW, gw_tot), (cx.LD, cx.now(cx.LD))], dv_wscale,
           inc=(cx.DV, 1))
    wscale_dv = cx.now(cx.DV)

    def cA(m, j):
        return Q["cols"][:, (m % 2) * 8 + j:(m % 2) * 8 + j + 1]

    def cB(m, j):
        return Q["cols"][:, 16 + (m % 2) * 8 + j:16 + (m % 2) * 8 + j + 1]

    ld_n = {}
    ldx_n = {}
    rounds_dv = {}
    rounds_pe = {}
    sbf_ac = {-1: cx.now(cx.AC)}
    sq1_ac = {}
    t4b_dv = {}
    ev_ac = {}
    y_pe = {}
    z_dv = {}
    sub_dv = {}
    mean_ac = {}
    st_n = {}
    ybanks = [cx.pb[0], cx.pb[1], cx.pc[0], cx.pc[1]]

    def loads(n):
        w = [(cx.LD, ld0)] if n < 2 else []
        if n >= 4:
            w += [(cx.AC, sq1_ac[n - 4]), (cx.DV, t4b_dv[n - 4])]
        if n >= 2:
            w += [(cx.DV, rounds_dv[n - 2])]
        movs = [(Q["opb"][n % 4][:], scr["outp"][n, :, :], {}),
                (Q["qch"][n % 2][:], scr["q"][n, :, :], {})]
        cx.dma("sync", w, movs, cx.LD)
        ld_n[n] = cx.now(cx.LD)

    def loadx(m):
        w = []
        if m >= 2:
            w = [(cx.DV, sub_dv[m - 2]), (cx.AC, mean_ac[m - 2])]
        cx.dma("sync", w,
               [(Q["zbuf"][m % 2][:], di["x_own"][m * 128:(m + 1) * 128, :], {})],
               cx.LD)
        ldx_n[m] = cx.now(cx.LD)

    loads(0)
    loads(1)
    loadx(0)
    loadx(1)

    def rounds(n):
        for r in range(4):
            w = [(cx.LD, ld_n[n])] if r == 0 else [(cx.PE, cx.now(cx.PE))]

            def dv_qh(v, n=n, r=r):
                qb = Q["qch"][n % 2]
                for ri in range(4):
                    h = r * 4 + ri
                    tt = v.tensor_tensor(Q["qhat"][ri][:],
                                         qb[:, h * 128:(h + 1) * 128],
                                         P["qdec"][:, h * 128:(h + 1) * 128],
                                         ALU.mult)
                return tt
            cx.add("vector", w, dv_qh, inc=(cx.DV, 1))

            wp = [(cx.DV, cx.now(cx.DV))]
            if r == 0:
                wp.append((cx.AC, sbf_ac[n - 1]))

            def pe_c(t, n=n, r=r):
                bank = cx.pa[r % 2]
                for ri in range(4):
                    h = r * 4 + ri
                    mm = t.matmul(bank[:, ri * 128:(ri + 1) * 128],
                                  Q["qhat"][ri][:],
                                  Q["Sbf"][:, h * 128:(h + 1) * 128],
                                  start=(ri == 0), stop=(ri == 3))
                return mm
            cx.add("tensor", wp, pe_c, inc=(cx.PE, 1))

            def dv_add(v, n=n, r=r):
                sl = Q["opb"][n % 4][:, r * 512:(r + 1) * 512]
                return v.scalar_tensor_tensor(sl, cx.pa[r % 2][:, :], 1.0, sl,
                                              ALU.mult, ALU.add)
            cx.add("vector", [(cx.PE, cx.now(cx.PE))], dv_add, inc=(cx.DV, 1))
        rounds_pe[n] = cx.now(cx.PE)

        def dv_sdec(v):
            return v.tensor_tensor(Q["Sin"][:], Q["Sin"][:], Q["rcrep"][:],
                                   ALU.mult)
        cx.add("vector", [(cx.PE, rounds_pe[n])], dv_sdec, inc=(cx.DV, 1))
        rounds_dv[n] = cx.now(cx.DV)

        def ac_sbf(sc):
            return sc.copy(Q["Sbf"][:], Q["Sin"][:])
        cx.add("scalar", [(cx.DV, rounds_dv[n])], ac_sbf, inc=(cx.AC, 1))
        sbf_ac[n] = cx.now(cx.AC)

    ISQE = float(1.0 / np.sqrt(E))

    def tail_a(m):
        def ac_sq1(sc, m=m):
            return sc.activation(Q["Dd"][:], Q["opb"][m % 4][:], AF.Square,
                                 scale=ISQE, accum_out=cA(m, 0))
        cx.add("scalar", [(cx.DV, rounds_dv[m])], ac_sq1, inc=(cx.AC, 1))
        sq1_ac[m] = cx.now(cx.AC)

        def dv_t2(v, m=m):
            return v.tensor_scalar(cA(m, 1), cA(m, 0), RMS_EPS, None, ALU.add)
        cx.add("vector", [(cx.AC, sq1_ac[m])], dv_t2, inc=(cx.DV, 1))

        def ac_sqrt1(sc, m=m):
            return sc.activation(cA(m, 2), cA(m, 1), AF.Sqrt)
        cx.add("scalar", [(cx.DV, cx.now(cx.DV))], ac_sqrt1, inc=(cx.AC, 1))

        def dv_t4(v, m=m):
            return v.reciprocal(cA(m, 3), cA(m, 2))
        cx.add("vector", [(cx.AC, cx.now(cx.AC))], dv_t4, inc=(cx.DV, 1))

        def dv_t4b(v, m=m):
            return v.tensor_scalar(Q["rmsbf"][m % 2][:], Q["opb"][m % 4][:],
                                   cA(m, 3), None, ALU.mult)
        cx.add("vector", [(cx.DV, cx.now(cx.DV))], dv_t4b, inc=(cx.DV, 1))
        t4b_dv[m] = cx.now(cx.DV)

        for gtp in range(2):
            def pe_t(t, m=m, gtp=gtp):
                for ri in range(8):
                    ti = gtp * 8 + ri
                    tr = t.transpose(cx.ptb[:, ri * 128:(ri + 1) * 128],
                                     Q["rmsbf"][m % 2][:, ti * 128:(ti + 1) * 128],
                                     P["identB"][:])
                return tr
            cx.add("tensor", [(cx.DV, t4b_dv[m]), (cx.AC, cx.now(cx.AC))],
                   pe_t, inc=(cx.PE, 1))

            def ac_t(sc, m=m, gtp=gtp):
                return sc.copy(Q["rmsfm"][:, gtp * 1024:(gtp + 1) * 1024],
                               cx.ptb[:, :])
            wv = [(cx.PE, cx.now(cx.PE))]
            if m >= 1:
                wv.append((cx.PE, y_pe[m - 1]))
            cx.add("scalar", wv, ac_t, inc=(cx.AC, 1))
        ev_ac[m] = cx.now(cx.AC)

        wy = [(cx.AC, ev_ac[m])]
        if m == 0:
            wy.append((cx.DV, wscale_dv))
        if m >= 1:
            wy.append((cx.DV, z_dv[m - 1]))

        def pe_y(t, m=m):
            for db in range(4):
                for ti in range(FT):
                    mm = t.matmul(ybanks[db][:, :],
                                  Q["rmsfm"][:, ti * 128:(ti + 1) * 128],
                                  Q["wbuf"][:, ti * E + db * 512:
                                            ti * E + db * 512 + 512],
                                  start=(ti == 0), stop=(ti == FT - 1))
            return mm
        cx.add("tensor", wy, pe_y, inc=(cx.PE, 1))
        y_pe[m] = cx.now(cx.PE)

    def tail_b(m):
        def dv_z(v, m=m):
            for db in range(4):
                sl = Q["zbuf"][m % 2][:, db * 512:(db + 1) * 512]
                st = v.scalar_tensor_tensor(sl, ybanks[db][:, :], 1.0, sl,
                                            ALU.mult, ALU.add)
            return st
        cx.add("vector", [(cx.PE, y_pe[m]), (cx.LD, ldx_n[m])], dv_z,
               inc=(cx.DV, 1))
        z_dv[m] = cx.now(cx.DV)

        def ac_mean(sc, m=m):
            return sc.activation(Q["Dd"][:], Q["zbuf"][m % 2][:], AF.Copy,
                                 scale=float(1.0 / E), accum_out=cB(m, 0))
        cx.add("scalar", [(cx.DV, z_dv[m])], ac_mean, inc=(cx.AC, 1))
        mean_ac[m] = cx.now(cx.AC)

        def dv_sub(v, m=m):
            return v.tensor_scalar(Q["TZ"][:], Q["zbuf"][m % 2][:], cB(m, 0),
                                   None, ALU.subtract)
        cx.add("vector", [(cx.AC, mean_ac[m])], dv_sub, inc=(cx.DV, 1))
        sub_dv[m] = cx.now(cx.DV)

        def ac_sq2(sc, m=m):
            return sc.activation(Q["Dd"][:], Q["TZ"][:], AF.Square,
                                 scale=ISQE, accum_out=cB(m, 1))
        cx.add("scalar", [(cx.DV, sub_dv[m])], ac_sq2, inc=(cx.AC, 1))

        def dv_v2(v, m=m):
            return v.tensor_scalar(cB(m, 2), cB(m, 1), LN_EPS, None, ALU.add)
        cx.add("vector", [(cx.AC, cx.now(cx.AC))], dv_v2, inc=(cx.DV, 1))

        def ac_sqrt2(sc, m=m):
            return sc.activation(cB(m, 3), cB(m, 2), AF.Sqrt)
        cx.add("scalar", [(cx.DV, cx.now(cx.DV))], ac_sqrt2, inc=(cx.AC, 1))

        wx = [(cx.AC, cx.now(cx.AC))]
        if m >= 1:
            wx.append((cx.ST, st_n[m - 1]))

        def dv_x1a(v, m=m):
            return v.reciprocal(cB(m, 4), cB(m, 3))
        cx.add("vector", wx, dv_x1a, inc=(cx.DV, 1))

        def dv_x1b(v, m=m):
            return v.tensor_scalar(Q["x1b"][:], Q["TZ"][:], cB(m, 4), None,
                                   ALU.mult)
        cx.add("vector", [(cx.DV, cx.now(cx.DV))], dv_x1b, inc=(cx.DV, 1))

        def ac_hi(sc):
            return sc.copy(Q["hibuf"][:], Q["x1b"][:])
        cx.add("scalar", [(cx.DV, cx.now(cx.DV))], ac_hi, inc=(cx.AC, 1))

        def dv_lo(v):
            return v.tensor_tensor(Q["lobuf"][:], Q["x1b"][:], Q["hibuf"][:],
                                   ALU.subtract)
        cx.add("vector", [(cx.AC, cx.now(cx.AC))], dv_lo, inc=(cx.DV, 1))

        rows = slice(m * 128, (m + 1) * 128)
        cx.dma("sync", [(cx.DV, cx.now(cx.DV))],
               [(scr["x1f"][rows, :], Q["x1b"][:], {}),
                (scr["x1hi"][rows, :], Q["hibuf"][:], {}),
                (scr["x1lo"][rows, :], Q["lobuf"][:], {})], cx.ST)
        st_n[m] = cx.now(cx.ST)

    for n in range(NCH + 2):
        if n < NCH:
            rounds(n)
        if 2 <= n <= NCH + 1:
            tail_b(n - 2)
        if 1 <= n <= NCH:
            tail_a(n - 1)
        if n + 2 < NCH:
            loads(n + 2)
        if 2 <= n < NCH:
            loadx(n)
    cx.barrier()


def phase_ffn(cx, debug=()):
    P, scr, di, nc = cx.P, cx.scr, cx.di, cx.nc

    for hp in range(2):
        with ExitStack() as hs:
            sbn = lambda n, shp, dt=F32: hs.enter_context(
                nc.sbuf_tensor(f"{n}_{hp}", shp, dt))
            G = {}
            G["acc"] = sbn("g_acc", [128, 8 * E])
            G["x1hi"] = sbn("g_x1hi", [128, FT * 1024], BF16)
            G["scores"] = sbn("g_scores", [128, 8 * H])
            G["gates"] = sbn("g_gates", [128, 8 * NE])
            G["gw"] = sbn("g_gw", [128, H])
            G["gm"] = sbn("g_gm", [128, H])
            G["gg"] = sbn("g_gg", [128, H])
            G["gtb"] = sbn("g_gtb", [18, 8 * 128], BF16)
            G["b1sb"] = sbn("g_b1sb", [128, NE * FB])
            G["b2sb"] = sbn("g_b2sb", [18, E], BF16)
            G["cols"] = sbn("g_cols", [128, 16])

            b1src = bass.AP(di["b1T"].tensor, 0,
                            [[FB, 128], [128 * FB, NE], [1, FB]])
            cx.dma("sync", cx.all_now(), [(G["b1sb"][:, :], b1src, {})], cx.LD)
            cx.dma("gpsimd", cx.all_now(),
                   [(G["b2sb"][:, :], di["b2cat"][:, :], {})], cx.GW)

            movs = []
            for tt in range(8):
                rows = slice(hp * 1024 + tt * 128, hp * 1024 + (tt + 1) * 128)
                movs.append((G["acc"][:, tt * E:(tt + 1) * E],
                             scr["x1f"][rows, :], {}))
            cx.dma("sync", [], movs, cx.LD)
            ld_acc = cx.now(cx.LD)

            def dv_a(v):
                for tt in range(8):
                    ts = v.tensor_scalar(G["acc"][:, tt * E:(tt + 1) * E],
                                         G["acc"][:, tt * E:(tt + 1) * E],
                                         2.0, None, ALU.mult)
                return ts
            cx.add("vector", [(cx.LD, ld_acc)], dv_a, inc=(cx.DV, 1))

            with ExitStack() as sa:
                sba = lambda n, shp, dt=F32: sa.enter_context(
                    nc.sbuf_tensor(f"{n}_{hp}", shp, dt))
                A = {}
                A["x1lo"] = sba("a_x1lo", [128, FT * 1024], BF16)
                A["centf"] = sba("a_centf", [128, FT * H])
                A["centhi"] = sba("a_centhi", [128, FT * H], BF16)
                A["centlo"] = sba("a_centlo", [128, FT * H], BF16)

                movs = [(G["x1hi"][:, t * 1024:(t + 1) * 1024],
                         scr["x1hi"][hp * 1024:(hp + 1) * 1024,
                                     t * 128:(t + 1) * 128],
                         {"transpose": True}) for t in range(FT)]
                movs += [(A["x1lo"][:, t * 1024:(t + 1) * 1024],
                          scr["x1lo"][hp * 1024:(hp + 1) * 1024,
                                      t * 128:(t + 1) * 128],
                          {"transpose": True}) for t in range(FT)]
                centsrc = bass.AP(di["centT"].tensor, 0,
                                  [[H, 128], [128 * H, FT], [1, H]])
                movs.append((A["centf"][:, :], centsrc, {}))
                cx.dma("sync", [], movs, cx.LD)
                ld_rt = cx.now(cx.LD)

                def dv_cent1(v):
                    return v.tensor_copy(A["centhi"][:], A["centf"][:])
                cx.add("vector", [(cx.LD, ld_rt)], dv_cent1, inc=(cx.DV, 1))

                def dv_cent2(v):
                    return v.tensor_tensor(A["centlo"][:], A["centf"][:],
                                           A["centhi"][:], ALU.subtract)
                cx.add("vector", [(cx.DV, cx.now(cx.DV))], dv_cent2,
                       inc=(cx.DV, 1))

                def dv_g1(v):
                    return v.memset(G["gates"][:], 1.0)
                cx.add("vector", [], dv_g1, inc=(cx.DV, 1))

                for tt in range(8):
                    def pe_r(t, tt=tt):
                        ops = [(G["x1hi"], A["centhi"]),
                               (A["x1lo"], A["centhi"]),
                               (G["x1hi"], A["centlo"])]
                        first = True
                        for xi, ci in ops:
                            for ti in range(FT):
                                mm = t.matmul(
                                    cx.ptf[:, 0:H],
                                    xi[:, ti * 1024 + tt * 128:
                                       ti * 1024 + tt * 128 + 128],
                                    ci[:, ti * H:(ti + 1) * H],
                                    start=first,
                                    stop=(xi is G["x1hi"] and ci is A["centlo"]
                                          and ti == FT - 1))
                                first = False
                        return mm
                    cx.add("tensor", [(cx.DV, cx.now(cx.DV))], pe_r,
                           inc=(cx.PE, 1))

                    def ac_sg(s, tt=tt):
                        return s.activation(G["scores"][:, tt * H:(tt + 1) * H],
                                            cx.ptf[:, 0:H], AF.Sigmoid)
                    cx.add("scalar", [(cx.PE, cx.now(cx.PE))], ac_sg,
                           inc=(cx.AC, 1))

                    def gstep(fn, w=None):
                        cx.add("vector",
                               w if w is not None else [(cx.DV, cx.now(cx.DV))],
                               fn, inc=(cx.DV, 1))
                    sc_sl = (tt * H, (tt + 1) * H)
                    gstep(lambda v, tt=tt: v.tensor_tensor(
                        G["scores"][:, tt * H:(tt + 1) * H],
                        G["scores"][:, tt * H:(tt + 1) * H],
                        P["biasrep"][:], ALU.add), [(cx.AC, cx.now(cx.AC))])
                    gstep(lambda v, tt=tt: v.tensor_copy(
                        G["gw"][:], G["scores"][:, tt * H:(tt + 1) * H]))
                    gstep(lambda v: v.reduce_max(G["cols"][:, 0:1], G["gw"][:],
                                                 axis=AX.X))
                    for _ in range(3):
                        gstep(lambda v: v.tensor_scalar(
                            G["gm"][:], G["gw"][:], G["cols"][:, 0:1], None,
                            ALU.is_equal))
                        gstep(lambda v: v.scalar_tensor_tensor(
                            G["gw"][:], G["gm"][:], -BIG, G["gw"][:],
                            ALU.mult, ALU.add))
                        gstep(lambda v: v.reduce_max(
                            G["cols"][:, 0:1], G["gw"][:], axis=AX.X))
                    gstep(lambda v, tt=tt: v.tensor_scalar(
                        G["gm"][:], G["scores"][:, tt * H:(tt + 1) * H],
                        G["cols"][:, 0:1], None, ALU.is_ge))
                    gstep(lambda v, tt=tt: v.tensor_tensor(
                        G["gg"][:], G["gm"][:],
                        G["scores"][:, tt * H:(tt + 1) * H], ALU.mult))
                    gstep(lambda v: v.tensor_scalar(
                        G["gw"][:], G["gg"][:], 1.0, 0.0, ALU.mult, ALU.add,
                        accum_out=G["cols"][:, 2:3]))
                    gstep(lambda v: v.tensor_scalar(
                        G["cols"][:, 3:4], G["cols"][:, 2:3], 1e-8, None,
                        ALU.max))
                    gstep(lambda v: v.reciprocal(G["cols"][:, 4:5],
                                                 G["cols"][:, 3:4]))
                    gstep(lambda v, tt=tt: v.tensor_scalar(
                        G["gates"][:, tt * NE + 2:(tt + 1) * NE],
                        G["gg"][:], G["cols"][:, 4:5], None, ALU.mult))

                    def pe_gt(t, tt=tt):
                        return t.transpose(cx.pa[0][0:NE, 0:128],
                                           G["gates"][:, tt * NE:(tt + 1) * NE],
                                           P["identF"][:])
                    cx.add("tensor", [(cx.DV, cx.now(cx.DV)),
                                      (cx.AC, cx.now(cx.AC))], pe_gt,
                           inc=(cx.PE, 1))

                    def ac_gt(s, tt=tt):
                        return s.copy(G["gtb"][:, tt * 128:(tt + 1) * 128],
                                      cx.pa[0][0:NE, 0:128])
                    cx.add("scalar", [(cx.PE, cx.now(cx.PE))], ac_gt,
                           inc=(cx.AC, 1))

                if debug and "gates" in debug and hp == 0:
                    cx.dma("sync", cx.all_now(),
                           [(cx.dbg["d_scores"][:, 0:8 * H], G["scores"][:], {}),
                            (cx.dbg["d_gates"][:, 0:8 * NE], G["gates"][:], {})],
                           cx.ST)
                cx.barrier()
                cx.flush()

            with ExitStack() as sb_:
                sbb = lambda n, shp, dt=F32: sb_.enter_context(
                    nc.sbuf_tensor(f"{n}_{hp}", shp, dt))
                X = {}
                X["hfm"] = sbb("b_hfm", [128, FB * 1024], BF16)
                X["w1s"] = [sbb(f"b_w1s{i}", [128, FT * 128], BF16)
                            for i in range(2)]
                X["w2q"] = [sbb(f"b_w2q{i}", [128, FB * 512], BF16)
                            for i in range(2)]

                hbank = [cx.pa[0], cx.pa[1], cx.pb[0], cx.pb[1]]
                ybank = [cx.pc[0], cx.pc[1]]
                pe_h = {}
                ac_h = {}
                dv_y = {}
                pe_y = {}
                gw_w1 = {}
                gw_w2 = {}
                pe_ylast = {-1: cx.now(cx.PE)}
                ac_router = cx.now(cx.AC)

                def load_w2q(e, dq):
                    w2src = bass.AP(di["w2cat"].tensor, e * DFF * E + dq * 512,
                                    [[E, 128], [128 * E, FB], [1, 512]])
                    key = (e, dq - 2) if dq >= 2 else (e - 1, dq + 2)
                    wv = [(cx.PE, pe_y[(key[0], key[1] * 8 + 7)])] \
                        if (key[0], key[1] * 8 + 7) in pe_y else []
                    cx.dma("gpsimd", wv, [(X["w2q"][dq % 2][:, :], w2src, {})],
                           cx.GW)
                    gw_w2[(e, dq)] = cx.now(cx.GW)

                for e in range(NE):
                    if e == 0:
                        load_w2q(0, 0)
                        load_w2q(0, 1)

                    for fb in range(FB):
                        w1src = bass.AP(di["w1cat"].tensor,
                                        (e * FB + fb) * 128 * FT * 128,
                                        [[FT * 128, 128], [1, FT * 128]])
                        wv = []
                        pk = (e, fb - 2, 1) if fb >= 2 else (e - 1, fb + FB - 2, 1)
                        if pk in pe_h:
                            wv = [(cx.PE, pe_h[pk])]
                        cx.dma("gpsimd", wv,
                               [(X["w1s"][fb % 2][:, :], w1src, {})], cx.GW)
                        gw_w1[(e, fb)] = cx.now(cx.GW)

                        for tb in range(2):
                            bank = hbank[(fb * 2 + tb) % 4]

                            def pe_hf(t, e=e, fb=fb, tb=tb, bank=bank):
                                for ti in range(FT):
                                    mm = t.matmul(
                                        bank[:, :],
                                        X["w1s"][fb % 2][:, ti * 128:(ti + 1) * 128],
                                        G["x1hi"][:, ti * 1024 + tb * 512:
                                                  ti * 1024 + tb * 512 + 512],
                                        start=(ti == 0), stop=(ti == FT - 1))
                                return mm
                            wv = [(cx.GW, gw_w1[(e, fb)])]
                            idx = fb * 2 + tb
                            if idx >= 4:
                                wv.append((cx.AC, ac_h[(e, idx - 4)]))
                            elif e >= 1:
                                wv.append((cx.AC, ac_h[(e - 1, 21)]))
                            else:
                                wv.append((cx.AC, ac_router))
                            if idx == 0 and e >= 1:
                                wv.append((cx.PE, pe_ylast[e - 1]))
                            cx.add("tensor", wv, pe_hf, inc=(cx.PE, 1))
                            pe_h[(e, fb, tb)] = cx.now(cx.PE)

                            def ac_hf(s, e=e, fb=fb, tb=tb, bank=bank):
                                return s.activation(
                                    X["hfm"][:, fb * 1024 + tb * 512:
                                             fb * 1024 + tb * 512 + 512],
                                    bank[:, :], AF.Relu,
                                    bias=G["b1sb"][:, e * FB + fb:
                                                   e * FB + fb + 1])
                            wv = [(cx.PE, pe_h[(e, fb, tb)])]
                            if idx == 0 and e >= 1:
                                wv.append((cx.PE, pe_ylast[e - 1]))
                            cx.add("scalar", wv, ac_hf, inc=(cx.AC, 1))
                            ac_h[(e, idx)] = cx.now(cx.AC)

                    for dq in range(4):
                        if dq + 2 < 4:
                            pass
                        for tt in range(8):
                            yi = dq * 8 + tt

                            def pe_yf(t, e=e, tt=tt, dq=dq, yi=yi):
                                bank = ybank[yi % 2]
                                for fb in range(FB):
                                    mm = t.matmul(
                                        bank[:, :],
                                        X["hfm"][:, fb * 1024 + tt * 128:
                                                 fb * 1024 + tt * 128 + 128],
                                        X["w2q"][dq % 2][:, fb * 512:
                                                         (fb + 1) * 512],
                                        start=(fb == 0), stop=(fb == FB - 1))
                                return mm
                            wv = [(cx.GW, gw_w2[(e, dq)]),
                                  (cx.AC, ac_h[(e, 21)])]
                            if yi >= 2:
                                wv.append((cx.DV, dv_y[(e, yi - 2)]))
                            elif e >= 1:
                                wv.append((cx.DV, dv_y[(e - 1, 30 + yi)]))
                            cx.add("tensor", wv, pe_yf, inc=(cx.PE, 1))
                            pe_y[(e, yi)] = cx.now(cx.PE)

                            def dv_yf(v, e=e, tt=tt, dq=dq, yi=yi):
                                bank = ybank[yi % 2]
                                return v.scalar_tensor_tensor(
                                    G["acc"][:, tt * E + dq * 512:
                                             tt * E + dq * 512 + 512],
                                    bank[:, :],
                                    G["gates"][:, tt * NE + e: tt * NE + e + 1],
                                    G["acc"][:, tt * E + dq * 512:
                                             tt * E + dq * 512 + 512],
                                    ALU.mult, ALU.add)
                            cx.add("vector", [(cx.PE, pe_y[(e, yi)])], dv_yf,
                                   inc=(cx.DV, 1))
                            dv_y[(e, yi)] = cx.now(cx.DV)
                        if dq + 2 < 4:
                            load_w2q(e, dq + 2)
                        elif e + 1 < NE:
                            load_w2q(e + 1, dq - 2)
                    pe_ylast[e] = cx.now(cx.PE)

                for yi in range(32):
                    tt, db = divmod(yi, 4)

                    def pe_b2(t, tt=tt, db=db, yi=yi):
                        return t.matmul(ybank[yi % 2][:, :],
                                        G["gtb"][:, tt * 128:(tt + 1) * 128],
                                        G["b2sb"][:, db * 512:(db + 1) * 512],
                                        start=True, stop=True)
                    wv = [(cx.DV, dv_y[(NE - 1, 30 + yi)] if yi < 2 else
                           cx.now(cx.DV))]
                    cx.add("tensor", wv, pe_b2, inc=(cx.PE, 1))

                    def dv_b2(v, tt=tt, db=db, yi=yi):
                        return v.tensor_tensor(
                            G["acc"][:, tt * E + db * 512:
                                     tt * E + db * 512 + 512],
                            G["acc"][:, tt * E + db * 512:
                                     tt * E + db * 512 + 512],
                            ybank[yi % 2][:, :], ALU.add)
                    cx.add("vector", [(cx.PE, cx.now(cx.PE))], dv_b2,
                           inc=(cx.DV, 1))

                cx.flush()
            with ExitStack() as sc_:
                sbc = lambda n, shp, dt=F32: sc_.enter_context(
                    nc.sbuf_tensor(f"{n}_{hp}", shp, dt))
                G["zz"] = [sbc(f"c_zz{i}", [128, E]) for i in range(3)]
                G["sq2"] = [sbc(f"c_sq2{i}", [128, E]) for i in range(2)]
                G["dsc"] = sbc("c_dsc", [128, E])
                G["outt"] = [sbc(f"c_ot{i}", [128, E]) for i in range(2)]
                G["lncols"] = sbc("c_lncols", [128, 32])
                st_t = {}
                sqr_ac = {}
                sq_ac = {}

                def cslot(tt, j):
                    b = (tt % 4) * 8 + j
                    return G["lncols"][:, b:b + 1]

                def stA(tt):
                    z = G["acc"][:, tt * E:(tt + 1) * E]

                    def dv_m(v, tt=tt, z=z):
                        return v.tensor_scalar(G["dsc"][:], z, 1.0 / E, 0.0,
                                               ALU.mult, ALU.add,
                                               accum_out=cslot(tt, 0))
                    w = [(cx.DV, cx.now(cx.DV))]
                    if tt >= 3 and (tt - 3) in sqr_ac:
                        w.append((cx.AC, sqr_ac[tt - 3]))
                    cx.add("vector", w, dv_m, inc=(cx.DV, 1))

                    def dv_sub(v, tt=tt, z=z):
                        return v.tensor_scalar(G["zz"][tt % 3][:], z,
                                               cslot(tt, 0), None, ALU.subtract)
                    cx.add("vector", [(cx.DV, cx.now(cx.DV))], dv_sub,
                           inc=(cx.DV, 1))

                    def ac_sqr(s, tt=tt):
                        return s.activation(G["sq2"][tt % 2][:],
                                            G["zz"][tt % 3][:], AF.Square)
                    cx.add("scalar", [(cx.DV, cx.now(cx.DV))], ac_sqr,
                           inc=(cx.AC, 1))
                    sqr_ac[tt] = cx.now(cx.AC)

                def stC(tt):
                    def dv_va(v, tt=tt):
                        return v.tensor_scalar(G["dsc"][:], G["sq2"][tt % 2][:],
                                               1.0 / E, 0.0, ALU.mult, ALU.add,
                                               accum_out=cslot(tt, 1))
                    cx.add("vector", [(cx.AC, sqr_ac[tt])], dv_va,
                           inc=(cx.DV, 1))

                    def dv_vb(v, tt=tt):
                        return v.tensor_scalar(cslot(tt, 2), cslot(tt, 1),
                                               LN_EPS, None, ALU.add)
                    cx.add("vector", [(cx.DV, cx.now(cx.DV))], dv_vb,
                           inc=(cx.DV, 1))

                    def ac_sq(s, tt=tt):
                        return s.activation(cslot(tt, 3), cslot(tt, 2), AF.Sqrt)
                    cx.add("scalar", [(cx.DV, cx.now(cx.DV))], ac_sq,
                           inc=(cx.AC, 1))
                    sq_ac[tt] = cx.now(cx.AC)

                def stE(tt):
                    def dv_oa(v, tt=tt):
                        return v.reciprocal(cslot(tt, 4), cslot(tt, 3))
                    w = [(cx.AC, sq_ac[tt])]
                    if tt >= 2:
                        w.append((cx.ST, st_t[tt - 2]))
                    cx.add("vector", w, dv_oa, inc=(cx.DV, 1))

                    def dv_ob(v, tt=tt):
                        return v.tensor_scalar(G["outt"][tt % 2][:],
                                               G["zz"][tt % 3][:],
                                               cslot(tt, 4), None, ALU.mult)
                    cx.add("vector", [(cx.DV, cx.now(cx.DV))], dv_ob,
                           inc=(cx.DV, 1))
                    rows = slice(hp * 1024 + tt * 128,
                                 hp * 1024 + (tt + 1) * 128)
                    cx.dma("sync", [(cx.DV, cx.now(cx.DV))],
                           [(cx.out_d[rows, :], G["outt"][tt % 2][:], {})],
                           cx.ST)
                    st_t[tt] = cx.now(cx.ST)

                for tt in range(10):
                    if tt < 8:
                        stA(tt)
                    if 1 <= tt and tt - 1 < 8:
                        stC(tt - 1)
                    if 2 <= tt and tt - 2 < 8:
                        stE(tt - 2)
                cx.barrier()
                cx.flush()


def kernel(**inputs):
    in_maps = host_prep(inputs)
    nc = build_program(upto="full")
    res = run_bass_kernel_spmd(nc, in_maps, core_ids=list(range(8)))
    out = np.empty((2, 8192, E), np.float32)
    for c in range(8):
        out[c // 4, (c % 4) * SEG:(c % 4 + 1) * SEG, :] = res.results[c]["out"]
    return out


# version 04
# speedup vs baseline: 1.1017x; 1.1017x over previous
from contextlib import ExitStack

import numpy as np
import ml_dtypes

import concourse.bass as bass
import concourse.mybir as mybir
from concourse.bass_utils import run_bass_kernel_spmd

F32 = mybir.dt.float32
BF16 = mybir.dt.bfloat16
I32 = mybir.dt.int32
AF = mybir.ActivationFunctionType
ALU = mybir.AluOpType
AX = mybir.AxisListType

E = 2048
H = 16
CH = 128
SEG = 2048
NCH = SEG // CH
FT = E // 128
DFF = 1408
FB = DFF // 128
NE = 18
RMS_EPS = 1e-5
LN_EPS = 1e-5
MAGIC = 0x5A5A1234
BIG = 1.0e30

_slopes = np.array([2.0 ** (-0.5 * (i + 1)) for i in range(H)], dtype=np.float64)
_r = np.exp(-_slopes)
R_C = _r ** CH
R_SEG = _r ** SEG


def _rot_tables(t0):
    freq = np.zeros(64, dtype=np.float64)
    freq[:32] = (1.0 / 1024.0) ** np.linspace(0.0, 1.0, 32)
    t = np.arange(t0, t0 + SEG, dtype=np.float64)
    theta = freq[:, None] * t[None, :]
    cos, sin = np.cos(theta), np.sin(theta)
    Cf = np.concatenate([cos, cos], axis=0).astype(np.float32)
    Sf = np.concatenate([sin, -sin], axis=0).astype(np.float32)
    return np.ascontiguousarray(Cf), np.ascontiguousarray(Sf)


def host_prep(inputs):
    x = np.ascontiguousarray(np.asarray(inputs["L0_attn_x"], dtype=np.float32))
    wq = np.ascontiguousarray(np.asarray(inputs["L0_attn_Wq"], dtype=np.float32))
    wk = np.ascontiguousarray(np.asarray(inputs["L0_attn_Wk"], dtype=np.float32))
    wv = np.ascontiguousarray(np.asarray(inputs["L0_attn_Wv"], dtype=np.float32))
    wo = np.ascontiguousarray(np.asarray(inputs["L0_attn_Wo"], dtype=np.float32))
    rms_w = np.asarray(inputs["L0_attn_rms_w"], dtype=np.float32)
    sw1 = np.asarray(inputs["L0_ffn_shared_w1"], dtype=np.float32)
    sb1 = np.asarray(inputs["L0_ffn_shared_b1"], dtype=np.float32)
    sw2 = np.asarray(inputs["L0_ffn_shared_w2"], dtype=np.float32)
    sb2 = np.asarray(inputs["L0_ffn_shared_b2"], dtype=np.float32)
    rw1 = np.asarray(inputs["L0_ffn_routed_w1"], dtype=np.float32)
    rb1 = np.asarray(inputs["L0_ffn_routed_b1"], dtype=np.float32)
    rw2 = np.asarray(inputs["L0_ffn_routed_w2"], dtype=np.float32)
    rb2 = np.asarray(inputs["L0_ffn_routed_b2"], dtype=np.float32)
    cent = np.asarray(inputs["L0_ffn_centroids"], dtype=np.float32)
    fbias = np.asarray(inputs["L0_ffn_bias"], dtype=np.float32)

    w1cat = np.concatenate([sw1, rw1], axis=0).reshape(NE, FT, 128, FB, 128)
    w1cat = np.ascontiguousarray(w1cat.transpose(0, 3, 2, 1, 4))
    w2cat = np.ascontiguousarray(np.concatenate([sw2, rw2], axis=0))
    b1T = np.ascontiguousarray(
        np.concatenate([sb1, rb1], axis=0).reshape(NE, FB, 128).transpose(0, 2, 1))
    b2cat = np.ascontiguousarray(np.concatenate([sb2, rb2], axis=0))
    centT = np.ascontiguousarray(cent.T)
    biasrep = np.ascontiguousarray(np.broadcast_to(fbias[None, :], (128, H)))
    rmswrep = np.ascontiguousarray(np.broadcast_to(rms_w[None, :], (128, E)))
    rmswT = np.ascontiguousarray(rms_w.reshape(FT, 128).T)

    i128 = np.arange(128, dtype=np.float64)
    qdecay = np.zeros((128, H * 128), dtype=np.float32)
    dmaskT = np.zeros((128, H * 128), dtype=np.float32)
    rcrep = np.zeros((128, H * 128), dtype=np.float32)
    kdec = np.zeros((128, H), dtype=np.float32)
    for h in range(H):
        qdecay[:, h * 128:(h + 1) * 128] = (_r[h] ** (i128 + 1.0))[None, :]
        diff = i128[None, :] - i128[:, None]
        dmaskT[:, h * 128:(h + 1) * 128] = np.where(
            diff >= 0, _r[h] ** np.maximum(diff, 0.0), 0.0)
        rcrep[:, h * 128:(h + 1) * 128] = R_C[h]
        kdec[:, h] = _r[h] ** (127.0 - i128)

    shared = dict(wq=wq, wk=wk, wv=wv, wo=wo, rmswrep=rmswrep, rmswT=rmswT,
                  w1cat=w1cat,
                  w2cat=w2cat, b1T=b1T, b2cat=b2cat, centT=centT,
                  biasrep=biasrep, qdecay=qdecay, dmaskT=dmaskT, rcrep=rcrep,
                  kdec=kdec,
                  identB=np.eye(128, dtype=ml_dtypes.bfloat16),
                  identF=np.eye(128, dtype=np.float32))

    in_maps = []
    for c in range(8):
        b, s = c // 4, c % 4
        x_own = np.ascontiguousarray(x[b, s * SEG:(s + 1) * SEG, :])
        es = {0: s, 1: s, 2: 0, 3: 1}[s]
        x_ex = np.ascontiguousarray(x[b, es * SEG:(es + 1) * SEG, :])
        Cown, Sown = _rot_tables(s * SEG)
        Cex, Sex = _rot_tables(es * SEG)
        coef = np.zeros((4, H), dtype=np.float64)
        if s == 1:
            coef[1, :] = 1.0
        elif s == 2:
            coef[2, :] = R_SEG
            coef[3, :] = 1.0
        elif s == 3:
            coef[1, :] = 1.0
            coef[2, :] = R_SEG
            coef[3, :] = R_SEG ** 2
        coefrep = np.ascontiguousarray(np.broadcast_to(
            coef.astype(np.float32).reshape(1, 4 * H), (128, 4 * H)))
        offs = np.zeros((1, 16), np.int32)
        offs[0, 0] = c % 2
        offs[0, 1] = 1 - (c % 2)
        m = dict(shared)
        m.update(x_own=x_own, x_ex=x_ex, rotc=Cown, rots=Sown,
                 rotc_ex=Cex, rots_ex=Sex, coefrep=coefrep, offs=offs)
        in_maps.append(m)
    return in_maps


ENGS = ("sync", "gpsimd", "tensor", "vector", "scalar")
PARANOID = False


class Cx:

    def __init__(self, nc, block):
        self.nc = nc
        self.block = block
        self.ct = {}
        self.q = {e: [] for e in ENGS}

    def now(self, sem):
        return self.ct.get(sem.name, 0)

    def bump(self, sem, n):
        self.ct[sem.name] = self.ct.get(sem.name, 0) + n
        return self.ct[sem.name]

    scope = None

    def add(self, eng, waits, fn, inc=None):
        for s, v in waits:
            assert v <= self.now(s), f"wait {s.name}>={v} exceeds appended {self.now(s)}"
        if PARANOID:
            waits = self.all_now()
        if self.scope is not None:
            fn0 = fn
            scope_name = self.scope

            def fn(e, fn0=fn0, scope_name=scope_name):
                with self.nc.named_scope(scope_name):
                    return fn0(e)
        self.q[eng].append((list(waits), fn, inc))
        if inc is not None:
            self.bump(inc[0], inc[1])

    def dma(self, eng, waits, movs, sem):
        def fn(e, movs=movs, sem=sem):
            last = None
            for o, i, kw in movs:
                last = e.dma_start(out=o, in_=i, **kw)
                last.then_inc(sem, 16)
            return None
        self.add(eng, waits, fn)
        self.bump(sem, 16 * len(movs))

    def all_now(self):
        return [(s, self.now(s)) for s in self.sems if self.now(s) > 0]

    def flush(self):
        for e in ENGS:
            steps = self.q[e]
            if not steps:
                continue
            self.q[e] = []

            def body(eng, steps=steps):
                for waits, fn, inc in steps:
                    for s, v in waits:
                        if v > 0:
                            eng.wait_ge(s, v)
                    inst = fn(eng)
                    if inc is not None:
                        inst.then_inc(inc[0], inc[1])
            getattr(self.block, e)(body)

    def barrier(self, engines=ENGS):
        waits = self.all_now()
        for e in engines:
            self.add(e, waits, lambda eng: None)


def build_program(upto="full", debug=()):
    nc = bass.Bass("TRN2", target_bir_lowering=False,
                   detect_race_conditions=False)
    di = {}

    def din(name, shape, dt=F32):
        di[name] = nc.dram_tensor(name, shape, dt, kind="ExternalInput").ap()

    din("x_own", [SEG, E]); din("x_ex", [SEG, E])
    din("wq", [E, E]); din("wk", [E, E]); din("wv", [E, E]); din("wo", [E, E])
    din("rotc", [128, SEG]); din("rots", [128, SEG])
    din("rotc_ex", [128, SEG]); din("rots_ex", [128, SEG])
    din("qdecay", [128, H * 128]); din("dmaskT", [128, H * 128])
    din("rcrep", [128, H * 128]); din("kdec", [128, H])
    din("rmswrep", [128, E]); din("rmswT", [128, FT])
    din("coefrep", [128, 4 * H])
    din("identB", [128, 128], BF16); din("identF", [128, 128])
    din("w1cat", [NE, FB, 128, FT, 128]); din("w2cat", [NE, DFF, E])
    din("b1T", [NE, 128, FB]); din("b2cat", [NE, E])
    din("centT", [E, H]); din("biasrep", [128, H])
    din("offs", [1, 16], I32)

    out_d = nc.dram_tensor("out", [SEG, E], F32, kind="ExternalOutput").ap()
    scr = dict(
        xbf=nc.dram_tensor("xbf_scr", [SEG, E], BF16, kind="Internal").ap(),
        q=nc.dram_tensor("q_scr", [NCH, 128, H * 128], BF16, kind="Internal").ap(),
        k=nc.dram_tensor("k_scr", [NCH, 128, H * 128], BF16, kind="Internal").ap(),
        v=nc.dram_tensor("v_scr", [NCH, 128, E], BF16, kind="Internal").ap(),
        outp=nc.dram_tensor("outp_scr", [NCH, 128, E], F32, kind="Internal").ap(),
        x1f=nc.dram_tensor("x1f_scr", [SEG, E], F32, kind="Internal").ap(),
        x1hi=nc.dram_tensor("x1hi_scr", [SEG, E], BF16, kind="Internal").ap(),
        x1lo=nc.dram_tensor("x1lo_scr", [SEG, E], BF16, kind="Internal").ap(),
        bex=nc.dram_tensor("bex_scr", [128, H * 128], F32, kind="Internal").ap(),
    )
    shm = nc.dram_tensor("shm", [2, 2 * 128, H * 128], F32, kind="Internal",
                         addr_space="Shared")
    flg = nc.dram_tensor("flg", [2, 16], I32, kind="Internal",
                         addr_space="Shared")

    dbg = {}
    if debug:
        def dout(name, shape, dt=F32):
            dbg[name] = nc.dram_tensor(name, shape, dt, kind="ExternalOutput").ap()
        if "qkv" in debug:
            dout("d_q", [NCH, 128, H * 128], BF16)
            dout("d_k", [NCH, 128, H * 128], BF16)
            dout("d_v", [NCH, 128, E], BF16)
        if "scan" in debug:
            dout("d_outp", [NCH, 128, E])
            dout("d_Aown", [128, H * 128])
            dout("d_Bex", [128, H * 128])
        if "sin" in debug:
            dout("d_Sin", [128, H * 128])
        if "attn" in debug:
            dout("d_attn", [NCH, 128, E])
        if "x1" in debug:
            dout("d_x1", [SEG, E])
        if "gates" in debug:
            dout("d_scores", [128, NCH * H])
            dout("d_gates", [128, NCH * NE])

    with ExitStack() as top:
        block = top.enter_context(nc.Block())
        cx = Cx(nc, block)
        cx.di, cx.scr, cx.dbg, cx.out_d = di, scr, dbg, out_d
        cx.shm, cx.flg = shm, flg

        sem = lambda n: top.enter_context(nc.semaphore(n))
        cx.LD = sem("sLD")
        cx.GW = sem("sGW")
        cx.PE = sem("sPE")
        cx.DV = sem("sDV")
        cx.AC = sem("sAC")
        cx.GP = sem("sGP")
        cx.ST = sem("sST")
        cx.PO = sem("sPO")
        cx.sems = [cx.LD, cx.GW, cx.PE, cx.DV, cx.AC, cx.GP, cx.ST]

        sb = lambda n, shp, dt=F32: top.enter_context(nc.sbuf_tensor(n, shp, dt))
        P = {}
        P["qdec"] = sb("c_qdec", [128, H * 128])
        P["kdec"] = sb("c_kdec", [128, H])
        P["coefr"] = sb("c_coefr", [128, 4 * H])
        P["biasrep"] = sb("c_biasr", [128, H])
        P["identB"] = sb("c_identB", [128, 128], BF16)
        P["identF"] = sb("c_identF", [128, 128])
        P["offs"] = sb("c_offs", [1, 16], I32)
        P["magic"] = sb("c_magic", [1, 16], I32)
        P["flagrd"] = sb("c_flagrd", [1, 16], I32)
        P["kvst"] = sb("c_kvst", [128, H * 128])
        P["kvbf"] = sb("c_kvbf", [128, H * 128], BF16)
        cx.P = P

        ps = lambda n, shp, dt=F32: top.enter_context(nc.psum_tensor(n, shp, dt))
        cx.pa = [ps("ps_a0", [128, 512]), ps("ps_a1", [128, 512])]
        cx.pb = [ps("ps_b0", [128, 512]), ps("ps_b1", [128, 512])]
        cx.pc = [ps("ps_c0", [128, 512]), ps("ps_c1", [128, 512])]
        cx.ptb = ps("ps_tb", [128, 1024], BF16)
        cx.ptf = ps("ps_tf", [128, 512])

        phase_consts(cx)

        with ExitStack() as front:
            fb = lambda n, shp, dt=F32: front.enter_context(nc.sbuf_tensor(n, shp, dt))
            F = {}
            F["xfm"] = fb("f_xfm", [128, FT * SEG], BF16)
            F["wbuf"] = fb("f_wbuf", [128, FT * E], BF16)
            F["rotC"] = fb("f_rotC", [128, SEG])
            F["rotS"] = fb("f_rotS", [128, SEG])
            F["ptile"] = [fb(f"f_pt{i}", [128, 512], BF16) for i in range(2)]
            F["rt_t"] = fb("f_rtt", [128, 512])
            F["rt_u"] = fb("f_rtu", [128, 512])
            F["vstage"] = [fb(f"f_vs{i}", [128, E], BF16) for i in range(2)]
            F["khat"] = [fb(f"f_kh{i}", [128, 128], BF16) for i in range(4)]
            cx.F = F

            with ExitStack() as s2s:
                F["kgr"] = s2s.enter_context(nc.sbuf_tensor("f_kgr", [128, 8 * 512], BF16))
                F["Bex"] = s2s.enter_context(nc.sbuf_tensor("f_Bex", [128, H * 128], F32))
                cx.scope = "ingest_ex"
                phase_ingest(cx, di["x_ex"], "rotc_ex", "rots_ex")
                cx.scope = "s2"
                if upto != "ingest":
                    phase_s2(cx)
                cx.flush()
            if upto != "ingest":
                cx.scope = "ingest_own"
                phase_ingest(cx, di["x_own"], "rotc", "rots")
                cx.scope = "proj"
                phase_proj_pipeline(cx)
            cx.flush()

        if debug and "qkv" in debug and upto not in ("ingest",):
            _dump(cx, dbg["d_q"], scr["q"])
            _dump(cx, dbg["d_k"], scr["k"])
            _dump(cx, dbg["d_v"], scr["v"])
        if debug and "scan" in debug and upto == "s2only":
            pass

        if upto in ("scan", "exch", "co", "full"):
            co_outer = ExitStack()
            sbn_o = lambda n, shp, dt=F32: co_outer.enter_context(
                nc.sbuf_tensor(n, shp, dt))
            Q = {}
            Q["wbuf"] = sbn_o("o_wbuf", [128, FT * E], BF16)
            Q["rmswT"] = sbn_o("o_rmswT", [128, FT])
            Q["Sin"] = sbn_o("o_Sin", [128, H * 128])
            Q["Sbf"] = sbn_o("o_Sbf", [128, H * 128], BF16)
            cx.Q = Q
            wosrc = bass.AP(di["wo"].tensor, 0,
                            [[2048, 128], [128 * 2048, FT], [1, 2048]])
            cx.dma("gpsimd", cx.all_now(), [(Q["wbuf"][:, :], wosrc, {})], cx.GW)
            cx.co_gw_tot = cx.now(cx.GW)
            cx.dma("sync", cx.all_now(),
                   [(Q["rmswT"][:], di["rmswT"][:, :], {})], cx.LD)
            with ExitStack() as sc:
                sbn = lambda n, shp, dt=F32: sc.enter_context(nc.sbuf_tensor(n, shp, dt))
                S = {}
                S["qch"] = [sbn(f"s_q{i}", [128, H * 128], BF16) for i in range(2)]
                S["kch"] = [sbn(f"s_k{i}", [128, H * 128], BF16) for i in range(2)]
                S["vch"] = [sbn(f"s_v{i}", [128, E], BF16) for i in range(2)]
                S["khat"] = [sbn(f"s_kh{i}", [128, 128], BF16) for i in range(4)]
                S["sprm"] = [sbn(f"s_sp{i}", [128, 128], BF16) for i in range(4)]
                S["qhat"] = [sbn(f"s_qh{i}", [128, 128], BF16) for i in range(4)]
                S["outbuf"] = [sbn(f"s_ob{i}", [128, E]) for i in range(2)]
                S["dmaskT"] = sbn("s_dmaskT", [128, H * 128])
                cx.S = S
                cx.scope = "scan"
                phase_scan(cx)
                cx.flush()
            if debug and "scan" in debug:
                _dump(cx, dbg["d_outp"], scr["outp"])
                _dump_sbuf(cx, dbg["d_Aown"], P["kvst"])
                _dump(cx, dbg["d_Bex"], scr["bex"])

        if upto in ("exch", "co", "full"):
            with ExitStack() as co:
                sbn = lambda n, shp, dt=F32: co.enter_context(nc.sbuf_tensor(n, shp, dt))
                cx.scope = "exch"
                phase_exchange(cx)
                if debug and "sin" in debug:
                    _dump_sbuf(cx, dbg["d_Sin"], Q["Sin"])
                Q["opb"] = [sbn(f"o_op{i}", [128, E]) for i in range(4)]
                Q["qch"] = [sbn(f"o_q{i}", [128, H * 128], BF16) for i in range(2)]
                Q["qhat"] = [sbn(f"o_qh{i}", [128, 128], BF16) for i in range(4)]
                Q["rmsbf"] = [sbn(f"o_rmsbf{i}", [128, E], BF16) for i in range(2)]
                Q["rmsfm"] = [sbn(f"o_rmsfm{i}", [128, H * 128], BF16)
                              for i in range(2)]
                Q["zbuf"] = [sbn(f"o_zbuf{i}", [128, E]) for i in range(2)]
                Q["TZ"] = [sbn(f"o_TZ{i}", [128, E]) for i in range(2)]
                Q["Dd"] = sbn("o_Dd", [128, E], BF16)
                Q["x1b"] = sbn("o_x1b", [128, E])
                Q["hibuf"] = sbn("o_hibuf", [128, E], BF16)
                Q["lobuf"] = sbn("o_lobuf", [128, E], BF16)
                Q["cols"] = sbn("o_cols", [128, 32])
                if upto in ("co", "full"):
                    cx.scope = "co"
                    phase_co(cx, debug)
                cx.flush()
            co_outer.close()
            if upto in ("co", "full") and debug and "x1" in debug:
                _dump(cx, dbg["d_x1"], scr["x1f"])

        if upto == "full":
            cx.scope = "ffn"
            phase_ffn(cx, debug)

        cx.add("gpsimd", cx.all_now(), lambda eng: None)
        cx.flush()

    nc.finalize()
    return nc



def phase_consts(cx):
    di, P = cx.di, cx.P
    movs = [(P[t][:], di[n][:, :], {}) for n, t in
            [("qdecay", "qdec"), ("kdec", "kdec"), ("coefrep", "coefr"),
             ("biasrep", "biasrep"),
             ("identB", "identB"), ("identF", "identF"), ("offs", "offs")]]
    cx.dma("sync", [], movs, cx.LD)

    def ve(v):
        v.memset(cx.P["magic"][:], MAGIC)
        v.memset(cx.P["kvst"][:], 0.0)
        return v.memset(cx.P["kvbf"][:], 0.0)
    cx.add("vector", [], ve, inc=(cx.DV, 1))
    cx.barrier()


def phase_ingest(cx, x_ap, rotc_name, rots_name):
    F, scr, di = cx.F, cx.scr, cx.di
    cx.dma("gpsimd", cx.all_now(), [(scr["xbf"][:, :], x_ap[:, :], {})], cx.GW)
    gw_tot = cx.now(cx.GW)
    movs = [(F["rotC"][:], di[rotc_name][:, :], {}),
            (F["rotS"][:], di[rots_name][:, :], {})]
    movs += [(F["xfm"][:, t * SEG:(t + 1) * SEG],
              scr["xbf"][:, t * 128:(t + 1) * 128],
              {"transpose": True}) for t in range(FT)]
    cx.dma("sync", [(cx.GW, gw_tot)], movs, cx.LD)
    cx.barrier()


def _rot_evac(v, cx, psrc, dst, g0):
    F = cx.F
    cs = slice(g0, g0 + 512)
    v.tensor_tensor(F["rt_t"][:], psrc[:, :], F["rotC"][:, cs], ALU.mult)
    v.tensor_tensor(F["rt_u"][0:64, :], psrc[64:128, :], F["rotS"][0:64, cs], ALU.mult)
    v.tensor_tensor(F["rt_u"][64:128, :], psrc[0:64, :], F["rotS"][64:128, cs], ALU.mult)
    return v.tensor_tensor(dst[:], F["rt_t"][:], F["rt_u"][:], ALU.add)


def _w_full_load(cx, wname):
    src = bass.AP(cx.di[wname].tensor, 0, [[2048, 128], [128 * 2048, FT], [1, 2048]])
    cx.dma("gpsimd", cx.all_now(), [(cx.F["wbuf"][:, :], src, {})], cx.GW)
    return cx.now(cx.GW)


def phase_proj_pipeline(cx):
    F, scr, di = cx.F, cx.scr, cx.di
    HFT = FT * 1024

    def load_half(wname, hh, slot, waits):
        src = bass.AP(di[wname].tensor, hh * 1024,
                      [[2048, 128], [128 * 2048, FT], [1, 1024]])
        cx.dma("gpsimd", waits,
               [(F["wbuf"][:, slot * HFT:(slot + 1) * HFT], src, {})], cx.GW)
        return cx.now(cx.GW)

    gw_q = [load_half("wq", 0, 0, cx.all_now()), load_half("wq", 1, 1, [])]
    gw_k = [None, None]
    gw_v = [None, None]
    pe_half_end = {}

    def fm_half(wname, hh, slot, dst_scr, gw_tot):
        pe0, dv0, st0 = cx.now(cx.PE), cx.now(cx.DV), cx.now(cx.ST)
        for j in range(32):
            grp, h8 = divmod(j, 8)
            h = hh * 8 + h8
            w = [(cx.GW, gw_tot)] if j == 0 else []
            if j >= 2:
                w.append((cx.DV, dv0 + j - 1))
            elif j < 2:
                w.append((cx.DV, dv0))

            def pe(t, j=j, grp=grp, h8=h8, slot=slot):
                bank = cx.pa[j % 2]
                for ti in range(FT):
                    mm = t.matmul(
                        bank[:, :],
                        F["wbuf"][:, slot * HFT + ti * 1024 + h8 * 128:
                                  slot * HFT + ti * 1024 + h8 * 128 + 128],
                        F["xfm"][:, ti * SEG + grp * 512:
                                 ti * SEG + grp * 512 + 512],
                        start=(ti == 0), stop=(ti == FT - 1))
                return mm
            cx.add("tensor", w, pe, inc=(cx.PE, 1))

            wv = [(cx.PE, pe0 + j + 1)]
            if j >= 2:
                wv.append((cx.ST, st0 + 16 * (j - 1)))
            elif j < 2:
                wv.append((cx.ST, st0))

            def ve(v, j=j, grp=grp):
                return _rot_evac(v, cx, cx.pa[j % 2], F["ptile"][j % 2],
                                 grp * 512)
            cx.add("vector", wv, ve, inc=(cx.DV, 1))

            off = (grp * 4) * (128 * H * 128) + h * 128
            dap = bass.AP(dst_scr.tensor, off,
                          [[H * 128, 128], [128 * H * 128, 4], [1, 128]])
            cx.dma("sync", [(cx.DV, dv0 + j + 1)],
                   [(dap, F["ptile"][j % 2][:], {})], cx.ST)
        return cx.now(cx.PE)

    e = fm_half("wq", 0, 0, scr["q"], gw_q[0])
    gw_k[0] = load_half("wk", 0, 0, [(cx.PE, e)])
    e2 = fm_half("wq", 1, 1, scr["q"], gw_q[1])
    gw_k[1] = load_half("wk", 1, 1, [(cx.PE, e2)])
    e3 = fm_half("wk", 0, 0, scr["k"], gw_k[0])
    gw_v[0] = load_half("wv", 0, 0, [(cx.PE, e3)])
    e4 = fm_half("wk", 1, 1, scr["k"], gw_k[1])
    gw_v[1] = load_half("wv", 1, 1, [(cx.PE, e4)])

    banks = [cx.pa[0], cx.pa[1], cx.pb[0], cx.pb[1]]
    pe0, ac0, st0, dv0 = cx.now(cx.PE), cx.now(cx.AC), cx.now(cx.ST), cx.now(cx.DV)
    for n in range(NCH):
        w = []
        if n == 0:
            w = [(cx.GW, gw_v[1]), (cx.DV, dv0)]
        if n >= 1:
            w.append((cx.AC, ac0 + n))

        def pe(t, n=n):
            for eb in range(4):
                slot, e2_ = divmod(eb, 2)
                for ti in range(FT):
                    mm = t.matmul(
                        banks[eb][:, :],
                        F["xfm"][:, ti * SEG + n * 128: ti * SEG + n * 128 + 128],
                        F["wbuf"][:, slot * HFT + ti * 1024 + e2_ * 512:
                                  slot * HFT + ti * 1024 + e2_ * 512 + 512],
                        start=(ti == 0), stop=(ti == FT - 1))
            return mm
        cx.add("tensor", w, pe, inc=(cx.PE, 1))

        wa = [(cx.PE, pe0 + n + 1)]
        if n >= 2:
            wa.append((cx.ST, st0 + 16 * (n - 1)))

        def ac(s, n=n):
            for eb in range(4):
                cp = s.copy(F["vstage"][n % 2][:, eb * 512:(eb + 1) * 512],
                            banks[eb][:, :])
            return cp
        cx.add("scalar", wa, ac, inc=(cx.AC, 1))

        cx.dma("sync", [(cx.AC, ac0 + n + 1)],
               [(scr["v"][n, :, :], F["vstage"][n % 2][:], {})], cx.ST)
    cx.barrier()


def phase_proj(cx, wname, dst_scr, rotary):
    F, pa = cx.F, cx.pa
    gw_tot = _w_full_load(cx, wname)
    pe0, dv0, st0 = cx.now(cx.PE), cx.now(cx.DV), cx.now(cx.ST)

    for j in range(64):
        grp, h = divmod(j, 16)
        w = [(cx.GW, gw_tot)] if j == 0 else []
        if j >= 2:
            w.append((cx.DV, dv0 + j - 1))

        def pe(t, j=j, grp=grp, h=h):
            bank = cx.pa[j % 2]
            for ti in range(FT):
                mm = t.matmul(bank[:, :],
                              F["wbuf"][:, ti * E + h * 128: ti * E + h * 128 + 128],
                              F["xfm"][:, ti * SEG + grp * 512:
                                       ti * SEG + grp * 512 + 512],
                              start=(ti == 0), stop=(ti == FT - 1))
            return mm
        cx.add("tensor", w, pe, inc=(cx.PE, 1))

        wv = [(cx.PE, pe0 + j + 1)]
        if j >= 2:
            wv.append((cx.ST, st0 + 16 * (j - 1)))

        def ve(v, j=j, grp=grp, rotary=rotary):
            if rotary:
                return _rot_evac(v, cx, cx.pa[j % 2], F["ptile"][j % 2], grp * 512)
            return v.tensor_copy(F["ptile"][j % 2][:], cx.pa[j % 2][:, :])
        cx.add("vector", wv, ve, inc=(cx.DV, 1))

        off = (grp * 4) * (128 * H * 128) + h * 128
        dap = bass.AP(dst_scr.tensor, off,
                      [[H * 128, 128], [128 * H * 128, 4], [1, 128]])
        cx.dma("sync", [(cx.DV, dv0 + j + 1)],
               [(dap, F["ptile"][j % 2][:], {})], cx.ST)
    cx.barrier()


def phase_projv(cx):
    F, scr = cx.F, cx.scr
    gw_tot = _w_full_load(cx, "wv")
    pe0, ac0, st0 = cx.now(cx.PE), cx.now(cx.AC), cx.now(cx.ST)
    banks = [cx.pa[0], cx.pa[1], cx.pb[0], cx.pb[1]]

    for n in range(NCH):
        w = [(cx.GW, gw_tot)] if n == 0 else []
        if n >= 1:
            w.append((cx.AC, ac0 + n))

        def pe(t, n=n):
            for eb in range(4):
                for ti in range(FT):
                    mm = t.matmul(banks[eb][:, :],
                                  F["xfm"][:, ti * SEG + n * 128: ti * SEG + n * 128 + 128],
                                  F["wbuf"][:, ti * E + eb * 512: ti * E + eb * 512 + 512],
                                  start=(ti == 0), stop=(ti == FT - 1))
            return mm
        cx.add("tensor", w, pe, inc=(cx.PE, 1))

        wa = [(cx.PE, pe0 + n + 1)]
        if n >= 2:
            wa.append((cx.ST, st0 + 16 * (n - 1)))

        def ac(s, n=n):
            for eb in range(4):
                cp = s.copy(F["vstage"][n % 2][:, eb * 512:(eb + 1) * 512],
                            banks[eb][:, :])
            return cp
        cx.add("scalar", wa, ac, inc=(cx.AC, 1))

        cx.dma("sync", [(cx.AC, ac0 + n + 1)],
               [(scr["v"][n, :, :], F["vstage"][n % 2][:], {})], cx.ST)
    cx.barrier()


def phase_s2(cx):
    F, P = cx.F, cx.P

    def ve0(v):
        return v.memset(F["Bex"][:], 0.0)
    cx.add("vector", [], ve0, inc=(cx.DV, 1))
    for hh in range(2):
        movs = []
        for wn, base in [("wk", 0), ("wv", FT * 1024)]:
            src = bass.AP(cx.di[wn].tensor, hh * 1024,
                          [[2048, 128], [128 * 2048, FT], [1, 1024]])
            movs.append((F["wbuf"][:, base:base + FT * 1024], src, {}))
        cx.dma("gpsimd", cx.all_now(), movs, cx.GW)
        gw_tot = cx.now(cx.GW)

        for grp in range(4):
            dv_k0 = cx.now(cx.DV)
            for j in range(8):
                w = [(cx.GW, gw_tot)] if (grp == 0 and j == 0) else []
                if j >= 2:
                    w.append((cx.DV, dv_k0 + j - 1))

                def pe(t, j=j, grp=grp, hh=hh):
                    bank = cx.pa[j % 2]
                    for ti in range(FT):
                        mm = t.matmul(
                            bank[:, :],
                            F["wbuf"][:, ti * 1024 + j * 128: ti * 1024 + j * 128 + 128],
                            F["xfm"][:, ti * SEG + grp * 512: ti * SEG + grp * 512 + 512],
                            start=(ti == 0), stop=(ti == FT - 1))
                    return mm
                cx.add("tensor", w, pe, inc=(cx.PE, 1))

                def ve(v, j=j, grp=grp):
                    return _rot_evac(v, cx, cx.pa[j % 2],
                                     F["kgr"][:, j * 512:(j + 1) * 512], grp * 512)
                cx.add("vector", [(cx.PE, cx.now(cx.PE))], ve, inc=(cx.DV, 1))
            dv_kdone = cx.now(cx.DV)

            for cc in range(4):
                n = grp * 4 + cc
                def pe_v(t, n=n):
                    for vb in range(2):
                        bank = cx.pb[vb]
                        for ti in range(FT):
                            mm = t.matmul(
                                bank[:, :],
                                F["xfm"][:, ti * SEG + n * 128: ti * SEG + n * 128 + 128],
                                F["wbuf"][:, FT * 1024 + ti * 1024 + vb * 512:
                                          FT * 1024 + ti * 1024 + vb * 512 + 512],
                                start=(ti == 0), stop=(ti == FT - 1))
                    return mm
                wv = [(cx.AC, cx.now(cx.AC))]
                cx.add("tensor", wv, pe_v, inc=(cx.PE, 1))

                def ac_v(s, cc=cc):
                    s.copy(F["vstage"][cc % 2][:, 0:512], cx.pb[0][:, :])
                    return s.copy(F["vstage"][cc % 2][:, 512:1024], cx.pb[1][:, :])
                cx.add("scalar", [(cx.PE, cx.now(cx.PE))], ac_v, inc=(cx.AC, 1))
                ac_v_tot = cx.now(cx.AC)

                for rr in range(2):
                    j0 = rr * 4

                    def pe_t(t, j0=j0, cc=cc, rr=rr):
                        for ri in range(4):
                            j = j0 + ri
                            tr = t.transpose(
                                cx.ptb[:, (rr * 4 + ri) * 128:(rr * 4 + ri) * 128 + 128],
                                F["kgr"][:, j * 512 + cc * 128: j * 512 + cc * 128 + 128],
                                P["identB"][:])
                        return tr
                    cx.add("tensor", [(cx.DV, dv_kdone), (cx.AC, cx.now(cx.AC))],
                           pe_t, inc=(cx.PE, 1))

                    def ac_k(s, j0=j0, rr=rr, hh=hh):
                        for ri in range(4):
                            h = hh * 8 + j0 + ri
                            cp = s.activation(
                                F["khat"][ri][:],
                                cx.ptb[:, (rr * 4 + ri) * 128:(rr * 4 + ri) * 128 + 128],
                                AF.Copy, scale=P["kdec"][:, h:h + 1])
                        return cp
                    cx.add("scalar", [(cx.PE, cx.now(cx.PE))], ac_k, inc=(cx.AC, 1))

                    def pe_s(t, j0=j0, rr=rr, cc=cc):
                        bank = cx.pc[rr]
                        for ri in range(4):
                            j = j0 + ri
                            mm = t.matmul(bank[:, ri * 128:(ri + 1) * 128],
                                          F["khat"][ri][:],
                                          F["vstage"][cc % 2][:, j * 128:(j + 1) * 128],
                                          start=(ri == 0), stop=(ri == 3))
                        return mm
                    cx.add("tensor",
                           [(cx.AC, cx.now(cx.AC)), (cx.DV, cx.now(cx.DV))],
                           pe_s, inc=(cx.PE, 1))

                    def dv_b(v, j0=j0, rr=rr, hh=hh):
                        bank = cx.pc[rr]
                        for ri in range(4):
                            h = hh * 8 + j0 + ri
                            st = v.scalar_tensor_tensor(
                                F["Bex"][:, h * 128:(h + 1) * 128],
                                F["Bex"][:, h * 128:(h + 1) * 128],
                                float(R_C[h]),
                                bank[:, ri * 128:(ri + 1) * 128],
                                ALU.mult, ALU.add)
                        return st
                    cx.add("vector", [(cx.PE, cx.now(cx.PE))], dv_b, inc=(cx.DV, 1))
    cx.dma("sync", [(cx.DV, cx.now(cx.DV))],
           [(cx.scr["bex"][:, :], F["Bex"][:], {})], cx.ST)
    cx.barrier()


def _dump(cx, dst_ap, src_scr):
    cx.dma("gpsimd", cx.all_now(), [(dst_ap[:], src_scr[:], {})], cx.GW)
    cx.barrier(("gpsimd",))


def _dump_sbuf(cx, dst_ap, src_sb):
    cx.dma("gpsimd", cx.all_now(), [(dst_ap[:, :], src_sb[:], {})], cx.GW)
    cx.barrier(("gpsimd",))


def phase_scan(cx):
    S, P, scr = cx.S, cx.P, cx.scr
    cx.dma("sync", [], [(S["dmaskT"][:], cx.di["dmaskT"][:, :], {})], cx.LD)
    cx.barrier(("sync", "tensor", "vector", "scalar", "gpsimd"))
    ld0, st0 = cx.now(cx.LD), cx.now(cx.ST)
    pe0, dv0, ac0, gp0 = (cx.now(cx.PE), cx.now(cx.DV), cx.now(cx.AC),
                          cx.now(cx.GP))

    def loads(n):
        w = []
        if n >= 2:
            w = [(cx.PE, pe0 + 2 * (4 * (n - 2) + 4)),
                 (cx.DV, dv0 + 3 * (4 * (n - 2) + 3) + 1)]
        movs = [(S["qch"][n % 2][:], scr["q"][n, :, :], {}),
                (S["kch"][n % 2][:], scr["k"][n, :, :], {}),
                (S["vch"][n % 2][:], scr["v"][n, :, :], {})]
        cx.dma("sync", w, movs, cx.LD)

    loads(0)
    loads(1)
    for n in range(NCH):
        for r in range(4):
            m = n * 4 + r
            w = [(cx.LD, ld0 + 48 * (n + 1))]
            if m >= 2:
                w += [(cx.DV, dv0 + 3 * (m - 2) + 1),
                      (cx.AC, ac0 + m - 1)]

            def pe_a(t, n=n, r=r):
                kb, qb = S["kch"][n % 2], S["qch"][n % 2]
                bank = cx.pa[r % 2]
                for ri in range(4):
                    h = r * 4 + ri
                    mm = t.matmul(bank[:, ri * 128:(ri + 1) * 128],
                                  kb[:, h * 128:(h + 1) * 128],
                                  qb[:, h * 128:(h + 1) * 128],
                                  start=(ri == 0), stop=(ri == 3))
                half = (r % 2) * 512
                for ri in range(4):
                    h = r * 4 + ri
                    tr = t.transpose(cx.ptb[:, half + ri * 128: half + ri * 128 + 128],
                                     kb[:, h * 128:(h + 1) * 128],
                                     P["identB"][:])
                return tr
            cx.add("tensor", w, pe_a, inc=(cx.PE, 1))

            def dv_a(v, n=n, r=r):
                bank = cx.pa[r % 2]
                for ri in range(4):
                    h = r * 4 + ri
                    v.tensor_tensor(S["sprm"][ri][:],
                                    bank[:, ri * 128:(ri + 1) * 128],
                                    S["dmaskT"][:, h * 128:(h + 1) * 128],
                                    ALU.mult)
                qb = S["qch"][n % 2]
                for ri in range(4):
                    h = r * 4 + ri
                    tt = v.tensor_tensor(S["qhat"][ri][:],
                                         qb[:, h * 128:(h + 1) * 128],
                                         P["qdec"][:, h * 128:(h + 1) * 128],
                                         ALU.mult)
                return tt
            cx.add("vector", [(cx.PE, pe0 + 2 * m + 1)], dv_a, inc=(cx.DV, 1))

            def ac_k(sc, r=r):
                half = (r % 2) * 512
                for ri in range(4):
                    h = r * 4 + ri
                    cp = sc.activation(S["khat"][ri][:],
                                       cx.ptb[:, half + ri * 128: half + ri * 128 + 128],
                                       AF.Copy, scale=P["kdec"][:, h:h + 1])
                return cp
            cx.add("scalar", [(cx.PE, pe0 + 2 * m + 1)], ac_k, inc=(cx.AC, 1))

            wb = [(cx.DV, dv0 + 3 * m + 1), (cx.AC, ac0 + m + 1)]
            if m >= 4:
                wb.append((cx.GP, gp0 + m - 3))

            def pe_b(t, n=n, r=r):
                vb = S["vch"][n % 2]
                ob = cx.pb[r % 2]
                for ri in range(4):
                    h = r * 4 + ri
                    t.matmul(ob[:, ri * 128:(ri + 1) * 128],
                             S["qhat"][ri][:],
                             P["kvbf"][:, h * 128:(h + 1) * 128],
                             start=(ri == 0), stop=False)
                    t.matmul(ob[:, ri * 128:(ri + 1) * 128],
                             S["sprm"][ri][:],
                             vb[:, h * 128:(h + 1) * 128],
                             start=False, stop=(ri == 3))
                kb_ = cx.pc[r % 2]
                for ri in range(4):
                    h = r * 4 + ri
                    mm = t.matmul(kb_[:, ri * 128:(ri + 1) * 128],
                                  S["khat"][ri][:],
                                  vb[:, h * 128:(h + 1) * 128],
                                  start=(ri == 0), stop=(ri == 3))
                return mm
            cx.add("tensor", wb, pe_b, inc=(cx.PE, 1))

            def dv_kv(v, r=r):
                kb_ = cx.pc[r % 2]
                for ri in range(4):
                    h = r * 4 + ri
                    st = v.scalar_tensor_tensor(
                        P["kvst"][:, h * 128:(h + 1) * 128],
                        P["kvst"][:, h * 128:(h + 1) * 128],
                        float(R_C[h]),
                        kb_[:, ri * 128:(ri + 1) * 128],
                        ALU.mult, ALU.add)
                return st
            cx.add("vector", [(cx.PE, pe0 + 2 * m + 2)], dv_kv, inc=(cx.DV, 1))

            wo_ = [(cx.PE, pe0 + 2 * m + 2)]
            if r == 0 and n >= 2:
                wo_.append((cx.ST, st0 + 16 * (n - 1)))

            def dv_out(v, n=n, r=r):
                return v.tensor_copy(S["outbuf"][n % 2][:, r * 512:(r + 1) * 512],
                                     cx.pb[r % 2][:, :])
            cx.add("vector", wo_, dv_out, inc=(cx.DV, 1))

            def gp_kv(g, r=r):
                for ri in range(4):
                    h = r * 4 + ri
                    cp = g.tensor_copy(P["kvbf"][:, h * 128:(h + 1) * 128],
                                       P["kvst"][:, h * 128:(h + 1) * 128])
                return cp
            cx.add("gpsimd", [(cx.DV, dv0 + 3 * m + 2)], gp_kv, inc=(cx.GP, 1))

        cx.dma("sync", [(cx.DV, dv0 + 3 * (4 * n + 4))],
               [(scr["outp"][n, :, :], S["outbuf"][n % 2][:], {})], cx.ST)
        if n + 2 < NCH:
            loads(n + 2)
    cx.barrier()


def phase_exchange(cx):
    P, Q = cx.P, cx.Q
    shm, flg = cx.shm, cx.flg
    waits = cx.all_now()
    gw_base = cx.now(cx.GW)

    def gp(g, gw_base=gw_base):
        with g.register("my") as my, g.register("pa") as pa, \
             g.register("fl") as fl, g.register("nd") as nd, \
             g.register("cnt") as cnt, g.register("off") as off:
            g.reg_load(my, cx.P["offs"][:1, 0:1])
            g.reg_load(pa, cx.P["offs"][:1, 1:2])
            g.reg_alu(off, my, 2 * 128 * H * 128, ALU.mult)
            g.dma_start(out=bass.AP(shm, off, [[H * 128, 128], [1, H * 128]]),
                        in_=P["kvst"][:]).then_inc(cx.GW, 16)
            g.reg_alu(off, off, 128 * H * 128, ALU.add)
            g.dma_start(out=bass.AP(shm, off, [[H * 128, 128], [1, H * 128]]),
                        in_=cx.scr["bex"][:, :]).then_inc(cx.GW, 16)
            g.wait_ge(cx.GW, gw_base + 32)
            g.reg_alu(off, my, 16, ALU.mult)
            g.dma_start(out=bass.AP(flg, off, [[16, 1], [1, 16]]),
                        in_=P["magic"][:]).then_inc(cx.GW, 16)
            g.wait_ge(cx.GW, gw_base + 48)
            g.reg_mov(nd, 1)
            g.reg_mov(cnt, 0)
            g.reg_alu(off, pa, 16, ALU.mult)
            with g.While(nd):
                g.reg_add(cnt, cnt, 16)
                g.dma_start(out=P["flagrd"][:],
                            in_=bass.AP(flg, off, [[16, 1], [1, 16]])
                            ).then_inc(cx.PO, 16)
                g.wait_ge(cx.PO, cnt)
                g.reg_load(fl, P["flagrd"][:1, 0:1])
                g.reg_alu(nd, fl, MAGIC, ALU.not_equal)
            g.reg_alu(off, pa, 2 * 128 * H * 128, ALU.mult)
            g.dma_start(out=cx.pbuf[:, 0:H * 128],
                        in_=bass.AP(shm, off, [[H * 128, 128], [1, H * 128]])
                        ).then_inc(cx.GW, 16)
            g.reg_alu(off, off, 128 * H * 128, ALU.add)
            g.dma_start(out=cx.pbuf[:, H * 128:2 * H * 128],
                        in_=bass.AP(shm, off, [[H * 128, 128], [1, H * 128]])
                        ).then_inc(cx.GW, 16)
        return None
    HK = H * 128
    with ExitStack() as ex:
        pbuf = ex.enter_context(cx.nc.sbuf_tensor("e_pbuf", [128, 3 * HK], F32))
        cx.pbuf = pbuf
        cx.add("gpsimd", waits, gp)
        cx.bump(cx.GW, 16 * 5)
        cx.dma("gpsimd", [], [(pbuf[:, 2 * HK:3 * HK], cx.scr["bex"][:, :], {})],
               cx.GW)
        gw_tot = cx.now(cx.GW)

        def dv0(v):
            return v.memset(Q["Sin"][:], 0.0)
        cx.add("vector", [(cx.GW, gw_tot)], dv0, inc=(cx.DV, 1))
        for si in range(4):
            def dvc(v, si=si):
                for h in range(H):
                    if si == 0:
                        src = P["kvst"][:, h * 128:(h + 1) * 128]
                    elif si == 1:
                        src = pbuf[:, h * 128:(h + 1) * 128]
                    elif si == 2:
                        src = pbuf[:, 2 * HK + h * 128: 2 * HK + (h + 1) * 128]
                    else:
                        src = pbuf[:, HK + h * 128: HK + (h + 1) * 128]
                    st = v.scalar_tensor_tensor(
                        Q["Sin"][:, h * 128:(h + 1) * 128], src,
                        P["coefr"][:, si * H + h: si * H + h + 1],
                        Q["Sin"][:, h * 128:(h + 1) * 128],
                        ALU.mult, ALU.add)
                return st
            cx.add("vector", [(cx.DV, cx.now(cx.DV))], dvc, inc=(cx.DV, 1))

        def ac(s):
            return s.copy(Q["Sbf"][:], Q["Sin"][:])
        cx.add("scalar", [(cx.DV, cx.now(cx.DV))], ac, inc=(cx.AC, 1))
        cx.barrier()
        cx.flush()


def phase_co(cx, debug=()):
    P, Q, scr, di = cx.P, cx.Q, cx.scr, cx.di
    gw_tot = cx.co_gw_tot
    ld0 = cx.now(cx.LD)

    def dv_wscale(v):
        for ti in range(FT):
            ts = v.tensor_scalar(Q["wbuf"][:, ti * E:(ti + 1) * E],
                                 Q["wbuf"][:, ti * E:(ti + 1) * E],
                                 Q["rmswT"][:, ti:ti + 1], None, ALU.mult)
        return ts
    cx.add("vector", [(cx.GW, gw_tot), (cx.LD, cx.now(cx.LD))], dv_wscale,
           inc=(cx.DV, 1))
    wscale_dv = cx.now(cx.DV)

    def cA(m, j):
        return Q["cols"][:, (m % 2) * 8 + j:(m % 2) * 8 + j + 1]

    def cB(m, j):
        return Q["cols"][:, 16 + (m % 2) * 8 + j:16 + (m % 2) * 8 + j + 1]

    ld_n, ldx_n = {}, {}
    rounds_dv, rounds_pe = {}, {}
    sbf_gp = {-1: cx.now(cx.GP)}
    sq1_ac, cp_ac = {}, {}
    t2_dv, sqrt1_ac, rcp_dv = {}, {}, {}
    tr_pe, ev_ac, y_pe = {}, {}, {}
    z_dv, mean_ac, sub_dv, sq2_ac, v2_dv, sqrt2_ac = {}, {}, {}, {}, {}, {}
    st_n = {}
    ybanks = [cx.pb[0], cx.pb[1], cx.pc[0], cx.pc[1]]
    ISQE = float(1.0 / np.sqrt(E))

    def loads(n):
        w = [(cx.LD, ld0)] if n < 2 else []
        if n >= 3:
            w += [(cx.AC, cp_ac[n - 3])]
        if n >= 2:
            w += [(cx.DV, rounds_dv[n - 2])]
        movs = [(Q["opb"][n % 4][:], scr["outp"][n, :, :], {}),
                (Q["qch"][n % 2][:], scr["q"][n, :, :], {})]
        cx.dma("sync", w, movs, cx.LD)
        ld_n[n] = cx.now(cx.LD)

    def loadx(m):
        w = []
        if m >= 2:
            w = [(cx.DV, sub_dv[m - 2]), (cx.AC, mean_ac[m - 2])]
        cx.dma("sync", w,
               [(Q["zbuf"][m % 2][:], di["x_own"][m * 128:(m + 1) * 128, :], {})],
               cx.LD)
        ldx_n[m] = cx.now(cx.LD)

    def rounds(n):
        for r in range(4):
            w = [(cx.LD, ld_n[n])] if r == 0 else [(cx.PE, cx.now(cx.PE))]

            def dv_qh(v, n=n, r=r):
                qb = Q["qch"][n % 2]
                for ri in range(4):
                    h = r * 4 + ri
                    tt = v.tensor_tensor(Q["qhat"][ri][:],
                                         qb[:, h * 128:(h + 1) * 128],
                                         P["qdec"][:, h * 128:(h + 1) * 128],
                                         ALU.mult)
                return tt
            cx.add("vector", w, dv_qh, inc=(cx.DV, 1))

            wp = [(cx.DV, cx.now(cx.DV))]
            if r == 0:
                wp.append((cx.GP, sbf_gp[n - 1]))

            def pe_c(t, n=n, r=r):
                bank = cx.pa[r % 2]
                for ri in range(4):
                    h = r * 4 + ri
                    mm = t.matmul(bank[:, ri * 128:(ri + 1) * 128],
                                  Q["qhat"][ri][:],
                                  Q["Sbf"][:, h * 128:(h + 1) * 128],
                                  start=(ri == 0), stop=(ri == 3))
                return mm
            cx.add("tensor", wp, pe_c, inc=(cx.PE, 1))

            def dv_add(v, n=n, r=r):
                sl = Q["opb"][n % 4][:, r * 512:(r + 1) * 512]
                return v.scalar_tensor_tensor(sl, cx.pa[r % 2][:, :], 1.0, sl,
                                              ALU.mult, ALU.add)
            cx.add("vector", [(cx.PE, cx.now(cx.PE))], dv_add, inc=(cx.DV, 1))
        rounds_pe[n] = cx.now(cx.PE)
        rounds_dv[n] = cx.now(cx.DV)

        def gp_sdec(g, n=n):
            for h in range(H):
                g.tensor_scalar(Q["Sin"][:, h * 128:(h + 1) * 128],
                                Q["Sin"][:, h * 128:(h + 1) * 128],
                                float(R_C[h]), None, ALU.mult)
            return g.tensor_copy(Q["Sbf"][:], Q["Sin"][:])
        cx.add("gpsimd", [(cx.PE, rounds_pe[n])], gp_sdec, inc=(cx.GP, 1))
        sbf_gp[n] = cx.now(cx.GP)

    def s_sq1(m):
        if "d_attn" in cx.dbg:
            cx.dma("sync", [(cx.DV, rounds_dv[m])],
                   [(cx.dbg["d_attn"][m, :, :], Q["opb"][m % 4][:], {})], cx.ST)
        def ac_sq1(sc, m=m):
            return sc.activation(Q["Dd"][:], Q["opb"][m % 4][:], AF.Square,
                                 scale=ISQE, accum_out=cA(m, 0))
        cx.add("scalar", [(cx.DV, rounds_dv[m])], ac_sq1, inc=(cx.AC, 1))
        sq1_ac[m] = cx.now(cx.AC)

        def ac_cp(sc, m=m):
            return sc.copy(Q["rmsbf"][m % 2][:], Q["opb"][m % 4][:])
        cx.add("scalar", [(cx.DV, rounds_dv[m])], ac_cp, inc=(cx.AC, 1))
        cp_ac[m] = cx.now(cx.AC)

    def s_t2(m):
        def dv_t2(v, m=m):
            return v.tensor_scalar(cA(m, 1), cA(m, 0), RMS_EPS, None, ALU.add)
        cx.add("vector", [(cx.AC, sq1_ac[m])], dv_t2, inc=(cx.DV, 1))
        t2_dv[m] = cx.now(cx.DV)

        def ac_sqrt1(sc, m=m):
            return sc.activation(cA(m, 2), cA(m, 1), AF.Sqrt)
        cx.add("scalar", [(cx.DV, t2_dv[m])], ac_sqrt1, inc=(cx.AC, 1))
        sqrt1_ac[m] = cx.now(cx.AC)

        for gtp in range(2):
            w = [(cx.AC, cp_ac[m])]
            if gtp == 1:
                w.append((cx.AC, cx.now(cx.AC)))
            elif m >= 1:
                w.append((cx.AC, ev_ac[m - 1]))

            def pe_t(t, m=m, gtp=gtp):
                for ri in range(8):
                    ti = gtp * 8 + ri
                    tr = t.transpose(cx.ptb[:, ri * 128:(ri + 1) * 128],
                                     Q["rmsbf"][m % 2][:, ti * 128:(ti + 1) * 128],
                                     P["identB"][:])
                return tr
            cx.add("tensor", w, pe_t, inc=(cx.PE, 1))

            def ac_t(sc, m=m, gtp=gtp):
                return sc.copy(Q["rmsfm"][m % 2][:, gtp * 1024:(gtp + 1) * 1024],
                               cx.ptb[:, :])
            wv = [(cx.PE, cx.now(cx.PE))]
            if m >= 2:
                wv.append((cx.PE, y_pe[m - 2]))
            cx.add("scalar", wv, ac_t, inc=(cx.AC, 1))
        ev_ac[m] = cx.now(cx.AC)
        tr_pe[m] = cx.now(cx.PE)

    def s_y(m):
        def dv_rcp(v, m=m):
            return v.reciprocal(cA(m, 3), cA(m, 2))
        cx.add("vector", [(cx.AC, sqrt1_ac[m])], dv_rcp, inc=(cx.DV, 1))
        rcp_dv[m] = cx.now(cx.DV)

        wy = [(cx.AC, ev_ac[m])]
        if m == 0:
            wy.append((cx.DV, wscale_dv))
        if m >= 1:
            wy.append((cx.DV, z_dv[m - 1]))

        def pe_y(t, m=m):
            for db in range(4):
                for ti in range(FT):
                    mm = t.matmul(ybanks[db][:, :],
                                  Q["rmsfm"][m % 2][:, ti * 128:(ti + 1) * 128],
                                  Q["wbuf"][:, ti * E + db * 512:
                                            ti * E + db * 512 + 512],
                                  start=(ti == 0), stop=(ti == FT - 1))
            return mm
        cx.add("tensor", wy, pe_y, inc=(cx.PE, 1))
        y_pe[m] = cx.now(cx.PE)

    def s_z(m):
        def dv_z(v, m=m):
            for db in range(4):
                sl = Q["zbuf"][m % 2][:, db * 512:(db + 1) * 512]
                st = v.scalar_tensor_tensor(sl, ybanks[db][:, :], cA(m, 3), sl,
                                            ALU.mult, ALU.add)
            return st
        cx.add("vector", [(cx.PE, y_pe[m]), (cx.LD, ldx_n[m]),
                          (cx.DV, rcp_dv[m])], dv_z, inc=(cx.DV, 1))
        z_dv[m] = cx.now(cx.DV)

        def ac_mean(sc, m=m):
            return sc.activation(Q["Dd"][:], Q["zbuf"][m % 2][:], AF.Copy,
                                 scale=float(1.0 / E), accum_out=cB(m, 0))
        cx.add("scalar", [(cx.DV, z_dv[m])], ac_mean, inc=(cx.AC, 1))
        mean_ac[m] = cx.now(cx.AC)

    def s_sub(m):
        def dv_sub(v, m=m):
            return v.tensor_scalar(Q["TZ"][m % 2][:], Q["zbuf"][m % 2][:],
                                   cB(m, 0), None, ALU.subtract)
        cx.add("vector", [(cx.AC, mean_ac[m])], dv_sub, inc=(cx.DV, 1))
        sub_dv[m] = cx.now(cx.DV)

        def ac_sq2(sc, m=m):
            return sc.activation(Q["Dd"][:], Q["TZ"][m % 2][:], AF.Square,
                                 scale=ISQE, accum_out=cB(m, 1))
        cx.add("scalar", [(cx.DV, sub_dv[m])], ac_sq2, inc=(cx.AC, 1))
        sq2_ac[m] = cx.now(cx.AC)

    def s_v2(m):
        def dv_v2(v, m=m):
            return v.tensor_scalar(cB(m, 2), cB(m, 1), LN_EPS, None, ALU.add)
        cx.add("vector", [(cx.AC, sq2_ac[m])], dv_v2, inc=(cx.DV, 1))
        v2_dv[m] = cx.now(cx.DV)

        def ac_sqrt2(sc, m=m):
            return sc.activation(cB(m, 3), cB(m, 2), AF.Sqrt)
        cx.add("scalar", [(cx.DV, v2_dv[m])], ac_sqrt2, inc=(cx.AC, 1))
        sqrt2_ac[m] = cx.now(cx.AC)

    def s_x1(m):
        wx = [(cx.AC, sqrt2_ac[m])]
        if m >= 1:
            wx.append((cx.ST, st_n[m - 1]))

        def dv_x1a(v, m=m):
            return v.reciprocal(cB(m, 4), cB(m, 3))
        cx.add("vector", wx, dv_x1a, inc=(cx.DV, 1))

        def dv_x1b(v, m=m):
            v.tensor_scalar(Q["x1b"][:], Q["TZ"][m % 2][:], cB(m, 4), None,
                            ALU.mult)
            return v.tensor_copy(Q["hibuf"][:], Q["x1b"][:])
        cx.add("vector", [(cx.DV, cx.now(cx.DV))], dv_x1b, inc=(cx.DV, 1))

        def dv_lo(v):
            return v.tensor_tensor(Q["lobuf"][:], Q["x1b"][:], Q["hibuf"][:],
                                   ALU.subtract)
        cx.add("vector", [(cx.DV, cx.now(cx.DV))], dv_lo, inc=(cx.DV, 1))

        rows = slice(m * 128, (m + 1) * 128)
        cx.dma("sync", [(cx.DV, cx.now(cx.DV))],
               [(scr["x1f"][rows, :], Q["x1b"][:], {}),
                (scr["x1hi"][rows, :], Q["hibuf"][:], {}),
                (scr["x1lo"][rows, :], Q["lobuf"][:], {})], cx.ST)
        st_n[m] = cx.now(cx.ST)

    loads(0)
    loads(1)
    loadx(0)
    loadx(1)
    for n in range(NCH + 7):
        if 0 <= n - 7 < NCH:
            s_x1(n - 7)
        if 0 <= n - 6 < NCH:
            s_v2(n - 6)
        if 0 <= n - 5 < NCH:
            s_sub(n - 5)
        if 0 <= n - 4 < NCH:
            s_z(n - 4)
        if 0 <= n - 3 < NCH:
            s_y(n - 3)
        if 0 <= n - 2 < NCH:
            s_t2(n - 2)
        if 0 <= n - 1 < NCH:
            s_sq1(n - 1)
        if n < NCH:
            rounds(n)
        if n + 2 < NCH:
            loads(n + 2)
        if 2 <= n - 3 < NCH:
            loadx(n - 3)
    cx.barrier()


def phase_ffn(cx, debug=()):
    P, scr, di, nc = cx.P, cx.scr, cx.di, cx.nc

    for hp in range(2):
        with ExitStack() as hs:
            sbn = lambda n, shp, dt=F32: hs.enter_context(
                nc.sbuf_tensor(f"{n}_{hp}", shp, dt))
            G = {}
            G["acc"] = sbn("g_acc", [128, 8 * E])
            G["x1hi"] = sbn("g_x1hi", [128, FT * 1024], BF16)
            G["scores"] = sbn("g_scores", [128, 8 * H])
            G["gates"] = sbn("g_gates", [128, 8 * NE])
            G["gw"] = sbn("g_gw", [128, H])
            G["gm"] = sbn("g_gm", [128, H])
            G["gg"] = sbn("g_gg", [128, H])
            G["gtb"] = sbn("g_gtb", [18, 8 * 128], BF16)
            G["b1sb"] = sbn("g_b1sb", [128, NE * FB])
            G["b2sb"] = sbn("g_b2sb", [18, E], BF16)
            G["cols"] = sbn("g_cols", [128, 16])

            b1src = bass.AP(di["b1T"].tensor, 0,
                            [[FB, 128], [128 * FB, NE], [1, FB]])
            cx.dma("sync", cx.all_now(), [(G["b1sb"][:, :], b1src, {})], cx.LD)
            cx.dma("gpsimd", cx.all_now(),
                   [(G["b2sb"][:, :], di["b2cat"][:, :], {})], cx.GW)

            movs = []
            for tt in range(8):
                rows = slice(hp * 1024 + tt * 128, hp * 1024 + (tt + 1) * 128)
                movs.append((G["acc"][:, tt * E:(tt + 1) * E],
                             scr["x1f"][rows, :], {}))
            cx.dma("sync", [], movs, cx.LD)
            ld_acc = cx.now(cx.LD)

            def dv_a(v):
                for tt in range(8):
                    ts = v.tensor_scalar(G["acc"][:, tt * E:(tt + 1) * E],
                                         G["acc"][:, tt * E:(tt + 1) * E],
                                         2.0, None, ALU.mult)
                return ts
            cx.add("vector", [(cx.LD, ld_acc)], dv_a, inc=(cx.DV, 1))

            with ExitStack() as sa:
                sba = lambda n, shp, dt=F32: sa.enter_context(
                    nc.sbuf_tensor(f"{n}_{hp}", shp, dt))
                A = {}
                A["x1lo"] = sba("a_x1lo", [128, FT * 1024], BF16)
                A["centf"] = sba("a_centf", [128, FT * H])
                A["centhi"] = sba("a_centhi", [128, FT * H], BF16)
                A["centlo"] = sba("a_centlo", [128, FT * H], BF16)

                movs = [(G["x1hi"][:, t * 1024:(t + 1) * 1024],
                         scr["x1hi"][hp * 1024:(hp + 1) * 1024,
                                     t * 128:(t + 1) * 128],
                         {"transpose": True}) for t in range(FT)]
                movs += [(A["x1lo"][:, t * 1024:(t + 1) * 1024],
                          scr["x1lo"][hp * 1024:(hp + 1) * 1024,
                                      t * 128:(t + 1) * 128],
                          {"transpose": True}) for t in range(FT)]
                centsrc = bass.AP(di["centT"].tensor, 0,
                                  [[H, 128], [128 * H, FT], [1, H]])
                movs.append((A["centf"][:, :], centsrc, {}))
                cx.dma("sync", [], movs, cx.LD)
                ld_rt = cx.now(cx.LD)

                def dv_cent1(v):
                    return v.tensor_copy(A["centhi"][:], A["centf"][:])
                cx.add("vector", [(cx.LD, ld_rt)], dv_cent1, inc=(cx.DV, 1))

                def dv_cent2(v):
                    return v.tensor_tensor(A["centlo"][:], A["centf"][:],
                                           A["centhi"][:], ALU.subtract)
                cx.add("vector", [(cx.DV, cx.now(cx.DV))], dv_cent2,
                       inc=(cx.DV, 1))

                def dv_g1(v):
                    return v.memset(G["gates"][:], 1.0)
                cx.add("vector", [], dv_g1, inc=(cx.DV, 1))

                for tt in range(8):
                    def pe_r(t, tt=tt):
                        ops = [(G["x1hi"], A["centhi"]),
                               (A["x1lo"], A["centhi"]),
                               (G["x1hi"], A["centlo"])]
                        first = True
                        for xi, ci in ops:
                            for ti in range(FT):
                                mm = t.matmul(
                                    cx.ptf[:, 0:H],
                                    xi[:, ti * 1024 + tt * 128:
                                       ti * 1024 + tt * 128 + 128],
                                    ci[:, ti * H:(ti + 1) * H],
                                    start=first,
                                    stop=(xi is G["x1hi"] and ci is A["centlo"]
                                          and ti == FT - 1))
                                first = False
                        return mm
                    cx.add("tensor", [(cx.DV, cx.now(cx.DV))], pe_r,
                           inc=(cx.PE, 1))

                    def ac_sg(s, tt=tt):
                        return s.activation(G["scores"][:, tt * H:(tt + 1) * H],
                                            cx.ptf[:, 0:H], AF.Sigmoid)
                    cx.add("scalar", [(cx.PE, cx.now(cx.PE))], ac_sg,
                           inc=(cx.AC, 1))

                    def gstep(fn, w=None):
                        cx.add("vector",
                               w if w is not None else [(cx.DV, cx.now(cx.DV))],
                               fn, inc=(cx.DV, 1))
                    sc_sl = (tt * H, (tt + 1) * H)
                    gstep(lambda v, tt=tt: v.tensor_tensor(
                        G["scores"][:, tt * H:(tt + 1) * H],
                        G["scores"][:, tt * H:(tt + 1) * H],
                        P["biasrep"][:], ALU.add), [(cx.AC, cx.now(cx.AC))])
                    gstep(lambda v, tt=tt: v.tensor_copy(
                        G["gw"][:], G["scores"][:, tt * H:(tt + 1) * H]))
                    gstep(lambda v: v.reduce_max(G["cols"][:, 0:1], G["gw"][:],
                                                 axis=AX.X))
                    for _ in range(3):
                        gstep(lambda v: v.tensor_scalar(
                            G["gm"][:], G["gw"][:], G["cols"][:, 0:1], None,
                            ALU.is_equal))
                        gstep(lambda v: v.scalar_tensor_tensor(
                            G["gw"][:], G["gm"][:], -BIG, G["gw"][:],
                            ALU.mult, ALU.add))
                        gstep(lambda v: v.reduce_max(
                            G["cols"][:, 0:1], G["gw"][:], axis=AX.X))
                    gstep(lambda v, tt=tt: v.tensor_scalar(
                        G["gm"][:], G["scores"][:, tt * H:(tt + 1) * H],
                        G["cols"][:, 0:1], None, ALU.is_ge))
                    gstep(lambda v, tt=tt: v.tensor_tensor(
                        G["gg"][:], G["gm"][:],
                        G["scores"][:, tt * H:(tt + 1) * H], ALU.mult))
                    gstep(lambda v: v.tensor_scalar(
                        G["gw"][:], G["gg"][:], 1.0, 0.0, ALU.mult, ALU.add,
                        accum_out=G["cols"][:, 2:3]))
                    gstep(lambda v: v.tensor_scalar(
                        G["cols"][:, 3:4], G["cols"][:, 2:3], 1e-8, None,
                        ALU.max))
                    gstep(lambda v: v.reciprocal(G["cols"][:, 4:5],
                                                 G["cols"][:, 3:4]))
                    gstep(lambda v, tt=tt: v.tensor_scalar(
                        G["gates"][:, tt * NE + 2:(tt + 1) * NE],
                        G["gg"][:], G["cols"][:, 4:5], None, ALU.mult))

                    def pe_gt(t, tt=tt):
                        return t.transpose(cx.pa[0][0:NE, 0:128],
                                           G["gates"][:, tt * NE:(tt + 1) * NE],
                                           P["identF"][:])
                    cx.add("tensor", [(cx.DV, cx.now(cx.DV)),
                                      (cx.AC, cx.now(cx.AC))], pe_gt,
                           inc=(cx.PE, 1))

                    def ac_gt(s, tt=tt):
                        return s.copy(G["gtb"][:, tt * 128:(tt + 1) * 128],
                                      cx.pa[0][0:NE, 0:128])
                    cx.add("scalar", [(cx.PE, cx.now(cx.PE))], ac_gt,
                           inc=(cx.AC, 1))

                if debug and "gates" in debug and hp == 0:
                    cx.dma("sync", cx.all_now(),
                           [(cx.dbg["d_scores"][:, 0:8 * H], G["scores"][:], {}),
                            (cx.dbg["d_gates"][:, 0:8 * NE], G["gates"][:], {})],
                           cx.ST)
                cx.barrier()
                cx.flush()

            with ExitStack() as sb_:
                sbb = lambda n, shp, dt=F32: sb_.enter_context(
                    nc.sbuf_tensor(f"{n}_{hp}", shp, dt))
                X = {}
                X["hfm"] = sbb("b_hfm", [128, FB * 1024], BF16)
                X["w1s"] = [sbb(f"b_w1s{i}", [128, FT * 128], BF16)
                            for i in range(2)]
                X["w2q"] = [sbb(f"b_w2q{i}", [128, FB * 512], BF16)
                            for i in range(2)]

                hbank = [cx.pa[0], cx.pa[1], cx.pb[0], cx.pb[1]]
                ybank = [cx.pc[0], cx.pc[1]]
                pe_h = {}
                ac_h = {}
                dv_y = {}
                pe_y = {}
                gw_w1 = {}
                gw_w2 = {}
                pe_ylast = {-1: cx.now(cx.PE)}
                ac_router = cx.now(cx.AC)

                def load_w2q(e, dq):
                    w2src = bass.AP(di["w2cat"].tensor, e * DFF * E + dq * 512,
                                    [[E, 128], [128 * E, FB], [1, 512]])
                    key = (e, dq - 2) if dq >= 2 else (e - 1, dq + 2)
                    wv = [(cx.PE, pe_y[(key[0], key[1] * 8 + 7)])] \
                        if (key[0], key[1] * 8 + 7) in pe_y else []
                    cx.dma("gpsimd", wv, [(X["w2q"][dq % 2][:, :], w2src, {})],
                           cx.GW)
                    gw_w2[(e, dq)] = cx.now(cx.GW)

                for e in range(NE):
                    if e == 0:
                        load_w2q(0, 0)
                        load_w2q(0, 1)

                    for fb in range(FB):
                        w1src = bass.AP(di["w1cat"].tensor,
                                        (e * FB + fb) * 128 * FT * 128,
                                        [[FT * 128, 128], [1, FT * 128]])
                        wv = []
                        pk = (e, fb - 2, 1) if fb >= 2 else (e - 1, fb + FB - 2, 1)
                        if pk in pe_h:
                            wv = [(cx.PE, pe_h[pk])]
                        cx.dma("gpsimd", wv,
                               [(X["w1s"][fb % 2][:, :], w1src, {})], cx.GW)
                        gw_w1[(e, fb)] = cx.now(cx.GW)

                        for tb in range(2):
                            bank = hbank[(fb * 2 + tb) % 4]

                            def pe_hf(t, e=e, fb=fb, tb=tb, bank=bank):
                                for ti in range(FT):
                                    mm = t.matmul(
                                        bank[:, :],
                                        X["w1s"][fb % 2][:, ti * 128:(ti + 1) * 128],
                                        G["x1hi"][:, ti * 1024 + tb * 512:
                                                  ti * 1024 + tb * 512 + 512],
                                        start=(ti == 0), stop=(ti == FT - 1))
                                return mm
                            wv = [(cx.GW, gw_w1[(e, fb)])]
                            idx = fb * 2 + tb
                            if idx >= 4:
                                wv.append((cx.AC, ac_h[(e, idx - 4)]))
                            elif e >= 1:
                                wv.append((cx.AC, ac_h[(e - 1, 21)]))
                            else:
                                wv.append((cx.AC, ac_router))
                            if idx == 0 and e >= 1:
                                wv.append((cx.PE, pe_ylast[e - 1]))
                            cx.add("tensor", wv, pe_hf, inc=(cx.PE, 1))
                            pe_h[(e, fb, tb)] = cx.now(cx.PE)

                            def ac_hf(s, e=e, fb=fb, tb=tb, bank=bank):
                                return s.activation(
                                    X["hfm"][:, fb * 1024 + tb * 512:
                                             fb * 1024 + tb * 512 + 512],
                                    bank[:, :], AF.Relu,
                                    bias=G["b1sb"][:, e * FB + fb:
                                                   e * FB + fb + 1])
                            wv = [(cx.PE, pe_h[(e, fb, tb)])]
                            if idx == 0 and e >= 1:
                                wv.append((cx.PE, pe_ylast[e - 1]))
                            cx.add("scalar", wv, ac_hf, inc=(cx.AC, 1))
                            ac_h[(e, idx)] = cx.now(cx.AC)

                    for dq in range(4):
                        if dq + 2 < 4:
                            pass
                        for tt in range(8):
                            yi = dq * 8 + tt

                            def pe_yf(t, e=e, tt=tt, dq=dq, yi=yi):
                                bank = ybank[yi % 2]
                                for fb in range(FB):
                                    mm = t.matmul(
                                        bank[:, :],
                                        X["hfm"][:, fb * 1024 + tt * 128:
                                                 fb * 1024 + tt * 128 + 128],
                                        X["w2q"][dq % 2][:, fb * 512:
                                                         (fb + 1) * 512],
                                        start=(fb == 0), stop=(fb == FB - 1))
                                return mm
                            wv = [(cx.GW, gw_w2[(e, dq)]),
                                  (cx.AC, ac_h[(e, 21)])]
                            if yi >= 2:
                                wv.append((cx.DV, dv_y[(e, yi - 2)]))
                            elif e >= 1:
                                wv.append((cx.DV, dv_y[(e - 1, 30 + yi)]))
                            cx.add("tensor", wv, pe_yf, inc=(cx.PE, 1))
                            pe_y[(e, yi)] = cx.now(cx.PE)

                            def dv_yf(v, e=e, tt=tt, dq=dq, yi=yi):
                                bank = ybank[yi % 2]
                                return v.scalar_tensor_tensor(
                                    G["acc"][:, tt * E + dq * 512:
                                             tt * E + dq * 512 + 512],
                                    bank[:, :],
                                    G["gates"][:, tt * NE + e: tt * NE + e + 1],
                                    G["acc"][:, tt * E + dq * 512:
                                             tt * E + dq * 512 + 512],
                                    ALU.mult, ALU.add)
                            cx.add("vector", [(cx.PE, pe_y[(e, yi)])], dv_yf,
                                   inc=(cx.DV, 1))
                            dv_y[(e, yi)] = cx.now(cx.DV)
                        if dq + 2 < 4:
                            load_w2q(e, dq + 2)
                        elif e + 1 < NE:
                            load_w2q(e + 1, dq - 2)
                    pe_ylast[e] = cx.now(cx.PE)

                for yi in range(32):
                    tt, db = divmod(yi, 4)

                    def pe_b2(t, tt=tt, db=db, yi=yi):
                        return t.matmul(ybank[yi % 2][:, :],
                                        G["gtb"][:, tt * 128:(tt + 1) * 128],
                                        G["b2sb"][:, db * 512:(db + 1) * 512],
                                        start=True, stop=True)
                    wv = [(cx.DV, dv_y[(NE - 1, 30 + yi)] if yi < 2 else
                           cx.now(cx.DV))]
                    cx.add("tensor", wv, pe_b2, inc=(cx.PE, 1))

                    def dv_b2(v, tt=tt, db=db, yi=yi):
                        return v.tensor_tensor(
                            G["acc"][:, tt * E + db * 512:
                                     tt * E + db * 512 + 512],
                            G["acc"][:, tt * E + db * 512:
                                     tt * E + db * 512 + 512],
                            ybank[yi % 2][:, :], ALU.add)
                    cx.add("vector", [(cx.PE, cx.now(cx.PE))], dv_b2,
                           inc=(cx.DV, 1))

                cx.flush()
            with ExitStack() as sc_:
                sbc = lambda n, shp, dt=F32: sc_.enter_context(
                    nc.sbuf_tensor(f"{n}_{hp}", shp, dt))
                G["zz"] = [sbc(f"c_zz{i}", [128, E]) for i in range(3)]
                G["sq2"] = [sbc(f"c_sq2{i}", [128, E]) for i in range(2)]
                G["dsc"] = sbc("c_dsc", [128, E])
                G["outt"] = [sbc(f"c_ot{i}", [128, E]) for i in range(2)]
                G["lncols"] = sbc("c_lncols", [128, 32])
                st_t = {}
                sqr_ac = {}
                sq_ac = {}

                def cslot(tt, j):
                    b = (tt % 4) * 8 + j
                    return G["lncols"][:, b:b + 1]

                def stA(tt):
                    z = G["acc"][:, tt * E:(tt + 1) * E]

                    def dv_m(v, tt=tt, z=z):
                        return v.tensor_scalar(G["dsc"][:], z, 1.0 / E, 0.0,
                                               ALU.mult, ALU.add,
                                               accum_out=cslot(tt, 0))
                    w = [(cx.DV, cx.now(cx.DV))]
                    if tt >= 3 and (tt - 3) in sqr_ac:
                        w.append((cx.AC, sqr_ac[tt - 3]))
                    cx.add("vector", w, dv_m, inc=(cx.DV, 1))

                    def dv_sub(v, tt=tt, z=z):
                        return v.tensor_scalar(G["zz"][tt % 3][:], z,
                                               cslot(tt, 0), None, ALU.subtract)
                    cx.add("vector", [(cx.DV, cx.now(cx.DV))], dv_sub,
                           inc=(cx.DV, 1))

                    def ac_sqr(s, tt=tt):
                        return s.activation(G["sq2"][tt % 2][:],
                                            G["zz"][tt % 3][:], AF.Square)
                    cx.add("scalar", [(cx.DV, cx.now(cx.DV))], ac_sqr,
                           inc=(cx.AC, 1))
                    sqr_ac[tt] = cx.now(cx.AC)

                def stC(tt):
                    def dv_va(v, tt=tt):
                        return v.tensor_scalar(G["dsc"][:], G["sq2"][tt % 2][:],
                                               1.0 / E, 0.0, ALU.mult, ALU.add,
                                               accum_out=cslot(tt, 1))
                    cx.add("vector", [(cx.AC, sqr_ac[tt])], dv_va,
                           inc=(cx.DV, 1))

                    def dv_vb(v, tt=tt):
                        return v.tensor_scalar(cslot(tt, 2), cslot(tt, 1),
                                               LN_EPS, None, ALU.add)
                    cx.add("vector", [(cx.DV, cx.now(cx.DV))], dv_vb,
                           inc=(cx.DV, 1))

                    def ac_sq(s, tt=tt):
                        return s.activation(cslot(tt, 3), cslot(tt, 2), AF.Sqrt)
                    cx.add("scalar", [(cx.DV, cx.now(cx.DV))], ac_sq,
                           inc=(cx.AC, 1))
                    sq_ac[tt] = cx.now(cx.AC)

                def stE(tt):
                    def dv_oa(v, tt=tt):
                        return v.reciprocal(cslot(tt, 4), cslot(tt, 3))
                    w = [(cx.AC, sq_ac[tt])]
                    if tt >= 2:
                        w.append((cx.ST, st_t[tt - 2]))
                    cx.add("vector", w, dv_oa, inc=(cx.DV, 1))

                    def dv_ob(v, tt=tt):
                        return v.tensor_scalar(G["outt"][tt % 2][:],
                                               G["zz"][tt % 3][:],
                                               cslot(tt, 4), None, ALU.mult)
                    cx.add("vector", [(cx.DV, cx.now(cx.DV))], dv_ob,
                           inc=(cx.DV, 1))
                    rows = slice(hp * 1024 + tt * 128,
                                 hp * 1024 + (tt + 1) * 128)
                    cx.dma("sync", [(cx.DV, cx.now(cx.DV))],
                           [(cx.out_d[rows, :], G["outt"][tt % 2][:], {})],
                           cx.ST)
                    st_t[tt] = cx.now(cx.ST)

                for tt in range(10):
                    if tt < 8:
                        stA(tt)
                    if 1 <= tt and tt - 1 < 8:
                        stC(tt - 1)
                    if 2 <= tt and tt - 2 < 8:
                        stE(tt - 2)
                cx.barrier()
                cx.flush()


def kernel(**inputs):
    in_maps = host_prep(inputs)
    nc = build_program(upto="full")
    res = run_bass_kernel_spmd(nc, in_maps, core_ids=list(range(8)))
    out = np.empty((2, 8192, E), np.float32)
    for c in range(8):
        out[c // 4, (c % 4) * SEG:(c % 4 + 1) * SEG, :] = res.results[c]["out"]
    return out
